# version 05
# baseline (speedup 1.0000x reference)
import contextlib
import math
import numpy as np
import concourse.bass as bass
import concourse.mybir as mybir
from concourse.bass_utils import run_bass_kernel_spmd

F32 = mybir.dt.float32
BF16 = mybir.dt.bfloat16
AF = mybir.ActivationFunctionType
ALU = mybir.AluOpType
AX = mybir.AxisListType

D = 1024
T_CORE = 2048
NTILE = 16
C_SQ = float(math.sqrt(1024.0) + 1e-6) ** 2
EPS1 = 1e-5
EPS2 = 1e-5
NEG = -1e30

ENGS = ("pe", "act", "dve", "pool", "sp")


def _small_key(k):
    return k.startswith(("STT", "STB", "ST2", "SMB", "ZT")) or ("STF" in k) or \
        ("_PG" in k) or ("_IND" in k)


class _Op:
    __slots__ = ("eng", "sname", "emit", "deps", "inc", "kind", "cum", "seq", "vol")

    def __init__(self, eng, sname, emit, deps, kind):
        self.vol = False
        self.eng = eng
        self.sname = sname
        self.emit = emit
        self.deps = deps
        self.kind = kind
        self.inc = kind in ("d", "cc")
        self.cum = 0
        self.seq = 0


class Sched:

    def __init__(self, strict=False):
        self.strict = strict
        self.ops = []
        self.by_sem = {}
        self.last_w = {}
        self.readers = {}
        self.barrier = []
        self.stream_last = {}

    def add(self, eng, emit, r=(), w=(), kind="c", extra=(), stream=None):
        sname = eng if kind == "c" else ("cc" if kind == "cc" else stream)
        assert sname is not None
        deps = list(self.barrier) + list(extra)
        if kind in ("d", "cc"):
            prev = self.stream_last.get(sname)
            if prev is not None:
                deps.append(prev)
        for k in r:
            op = self.last_w.get(k)
            if op is not None:
                deps.append(op)
        for k in w:
            op = self.last_w.get(k)
            if op is not None:
                deps.append(op)
            deps.extend(self.readers.get(k, ()))
        o = _Op(eng, sname, emit, deps, kind)
        o.vol = any(_small_key(k) for k in w)
        if kind in ("d", "cc"):
            self.stream_last[sname] = o
        for k in r:
            self.readers.setdefault(k, []).append(o)
        for k in w:
            self.last_w[k] = o
            self.readers[k] = []
        self.ops.append(o)
        lst = self.by_sem.setdefault(sname, [])
        o.cum = (lst[-1].cum if lst else 0) + (16 if kind == "d" else 1)
        lst.append(o)
        return o

    def phase_barrier(self):
        last = {}
        for o in self.ops:
            last[o.sname] = o
        self.barrier = list(last.values())

    def emit_all(self, nc):
        for o in self.ops:
            for d in o.deps:
                if d.kind == "c":
                    if self.strict or d.vol or d.eng != o.eng or o.kind != "c":
                        d.inc = True
                else:
                    d.inc = True
        for sname, lst in self.by_sem.items():
            cum = 0
            for o in lst:
                if o.inc:
                    cum += 16 if o.kind == "d" else 1
                o.cum = cum
        sems = {sname: nc.alloc_semaphore(f"s_{sname}") for sname in self.by_sem}
        per_eng = {e: [o for o in self.ops if o.eng == e] for e in ENGS}

        def runner(ename):
            def body(eng):
                hw = {}
                for o in per_eng[ename]:
                    need = {}
                    for d in o.deps:
                        if d.cum == 0:
                            continue
                        if (not self.strict) and (not d.vol) and d.kind == "c" \
                                and d.eng == ename and o.kind == "c":
                            continue
                        if d.cum > need.get(d.sname, 0):
                            need[d.sname] = d.cum
                    for s2 in sorted(need):
                        v = need[s2]
                        if v > hw.get(s2, 0):
                            eng.wait_ge(sems[s2], v)
                            hw[s2] = v
                    inst = o.emit(eng)
                    if o.inc:
                        assert inst is not None, f"inc-op on {ename} returned no instruction"
                        inst.then_inc(sems[o.sname], 16 if o.kind == "d" else 1)
            return body

        with nc.Block() as block:
            block.tensor(runner("pe"))
            block.scalar(runner("act"))
            block.vector(runner("dve"))
            block.gpsimd(runner("pool"))
            block.sync(runner("sp"))


def build_program(stop_phase=99, strict_sync=False):
    import os as _os
    nc = bass.Bass("TRN2", target_bir_lowering=False)
    S = Sched(strict=strict_sync)

    dI = lambda n, s: nc.dram_tensor(n, list(s), F32, kind="ExternalInput").ap()
    x_in = dI("x", (T_CORE, D))
    qkvT_d = dI("qkvT", (D, 3072))
    projT_d = dI("projT", (D, D))
    conv1T_d = dI("conv1T", (D, 2048))
    conv2T_d = dI("conv2T", (2048, D))
    scale1v = dI("scale1v", (128, 8))
    scale2v = dI("scale2v", (128, 8))
    gammaA_d = dI("gammaA", (1, D))
    projb_d = dI("projb", (1, D))
    conv1b_d = dI("conv1b", (128, 16))
    conv2b_d = dI("conv2b", (1, D))
    WqT_d = dI("WqT", (D, D))
    WkT_d = dI("WkT", (D, 512))
    WvT_d = dI("WvT", (D, 256))
    w1b_s = dI("w1b_s", (8, 8, 128, 512))
    w2b_s = dI("w2b_s", (32, 128, 1024))
    gammaF_d = dI("gammaF", (128, 8))
    betaF_d = dI("betaF", (128, 8))
    WgT_d = dI("WgT", (D, 8))
    bg_d = dI("bg", (1, 8))
    b1v_d = dI("b1v", (128, 8, 32))
    b2r_d = dI("b2r", (1, 8192))
    w1_s = dI("w1_s", (8, 8, 8, 128, 512))
    w2_s = dI("w2_s", (8, 32, 128, 1024))
    identf_d = dI("identf", (128, 128))
    masks_d = dI("masks", (128, 2, 256))
    ropeC_d = dI("ropeC", (17, 128, 32))
    ropeS_d = dI("ropeS", (17, 128, 32))
    cpool_d = dI("cpool", (128, 8))
    ones_d = dI("ones128", (1, 128))
    hsel_d = dI("halo_sel", (128, 8))

    y_out = nc.dram_tensor("y", [T_CORE, D], F32, kind="ExternalOutput").ap()

    kv_ar_in = nc.dram_tensor("kv_ar_in", [128, 520], F32, kind="Internal")
    kv_ar_out = nc.dram_tensor("kv_ar_out", [128, 520], F32, kind="Internal")
    ag_in = nc.dram_tensor("ag_in", [128, 1024], F32, kind="Internal")
    ag_out = nc.dram_tensor("ag_out", [1024, 1024], F32, kind="Internal")
    w1_st = nc.dram_tensor("w1_stash", [8, 8, 8, 128, 512], BF16, kind="Internal")
    w2_st = nc.dram_tensor("w2_stash", [8, 32, 128, 1024], BF16, kind="Internal")
    w1b_st = nc.dram_tensor("w1b_stash", [8, 8, 128, 512], BF16, kind="Internal")
    w2b_st = nc.dram_tensor("w2b_stash", [32, 128, 1024], BF16, kind="Internal")

    al = nc.alloc_sbuf_tensor
    X = al("X", [128, NTILE, D], F32)
    CP = al("CP", [128, 8], F32)
    IDENT = al("IDENT", [128, 128], F32)
    S1V = al("S1V", [128, 8], F32)
    S2V = al("S2V", [128, 8], F32)
    GF = al("GF", [128, 8], F32)
    BFv = al("BFv", [128, 8], F32)
    B1 = al("B1", [128, 8, 32], F32)
    ONE1 = al("ONE1", [1, 128], BF16)
    ONE1F = al("ONE1F", [1, 128], F32)
    WG = al("WG", [128, 8, 8], F32)
    BGR = al("BGR", [1, 8], F32)
    WTM = al("WTM", [128, NTILE, 8], F32)
    HSEL = al("HSEL", [128, 8], F32)

    PQ = [nc.alloc_psum_tensor(f"PQ{j}", [128, 2048], F32) for j in range(2)]
    PP = [PQ[0][:, 0:1024], PQ[0][:, 1024:2048], PQ[1][:, 0:1024], PQ[1][:, 1024:2048]]

    def bank(i):
        return PQ[i // 4][:, (i % 4) * 512:(i % 4) * 512 + 512]

    BK = [f"bk{i}" for i in range(8)]
    PKEY = [[f"bk{2 * j}", f"bk{2 * j + 1}"] for j in range(4)]
    ALLBK = [b for j in range(4) for b in PKEY[j]]

    def dma(eng_name, out_ap, in_ap, r=(), w=(), extra=(), stream="c0"):
        def em(eng, out_ap=out_ap, in_ap=in_ap):
            return eng.dma_start(out=out_ap, in_=in_ap)
        return S.add(eng_name, em, r=r, w=w, kind="d", extra=extra, stream=stream)

    def emit_output():
        yr = y_out.rearrange("(t p) c -> p t c", p=128)
        outops = []
        for i in range(4):
            o = dma("sp", yr[:, 4 * i:4 * i + 4, :], X[:, 4 * i:4 * i + 4, :],
                    r=[f"X{t}" for t in range(4 * i, 4 * i + 4)], w=[f"d_y{i}"],
                    stream="cy")
            outops.append(o)
        S.add("sp", lambda eng: eng.nop(), extra=outops)

    ctxR1 = contextlib.ExitStack()
    sbR1 = lambda n, s, dt: ctxR1.enter_context(nc.sbuf_tensor(n, list(s), dt, side="right"))
    GBC = sbR1("GBC", [128, D], BF16)
    XA = sbR1("XA", [128, NTILE, D], BF16)
    C2BG = sbR1("C2BG", [1, D], BF16)

    ctxR2 = contextlib.ExitStack()
    sbR2 = lambda n, s, dt: ctxR2.enter_context(nc.sbuf_tensor(n, list(s), dt, side="right"))
    QPHI = sbR2("QPHI", [128, 8, T_CORE], BF16)
    KVACC = sbR2("KVACC", [128, 8, 65], F32)
    PBG = sbR2("PBG", [1, D], BF16)
    _stdump = None
    if _os.environ.get("DEVTAP") == "A":
        STD = sbR2("STD", [128, NTILE, 8], F32)
        dbg_st = nc.dram_tensor("dbg_st", [128, NTILE, 8], F32, kind="ExternalOutput").ap()
        _stdump = (STD, dbg_st)

    ctxI = contextlib.ExitStack()
    sbI = lambda n, s, dt: ctxI.enter_context(nc.sbuf_tensor(n, list(s), dt))
    GAR = sbI("GAR", [1, D], F32)
    PBR = sbI("PBR", [1, D], F32)
    C2BR = sbI("C2BR", [1, D], F32)

    xr = x_in.rearrange("(t p) c -> p t c", p=128)
    for i in range(4):
        dma("sp", X[:, 4 * i:4 * i + 4, :], xr[:, 4 * i:4 * i + 4, :],
            w=[f"X{t}" for t in range(4 * i, 4 * i + 4)], stream="cx")
    dma("sp", CP[:, :], cpool_d, w=["CP"])
    dma("sp", IDENT[:, :], identf_d, w=["IDENT"])
    dma("sp", S1V[:, :], scale1v, w=["S1V"])
    dma("sp", S2V[:, :], scale2v, w=["S2V"])
    dma("sp", GF[:, :], gammaF_d, w=["GF"])
    dma("sp", BFv[:, :], betaF_d, w=["BFv"])
    dma("sp", B1[:, :, :], b1v_d, w=["B1"])
    dma("pool", ONE1[:, :], ones_d, w=["ONE1"], stream="p0")
    dma("sp", ONE1F[:, :], ones_d, w=["ONE1F"])
    dma("sp", WG[:, :, :], WgT_d.rearrange("(c p) e -> p c e", p=128), w=["WG"])
    dma("sp", BGR[:, :], bg_d, w=["BGR"])
    dma("sp", GAR[:, :], gammaA_d, w=["GAR"])
    dma("sp", PBR[:, :], projb_d, w=["PBR"])
    dma("sp", C2BR[:, :], conv2b_d, w=["C2BR"])
    dma("sp", HSEL[:, :], hsel_d, w=["HSEL"])

    if _os.environ.get("DEVTAP") == "X":
        dbg_x = nc.dram_tensor("dbg_x", [128, NTILE, D], F32, kind="ExternalOutput").ap()
        dma("sp", dbg_x, X[:, :, :], r=[f"X{t}" for t in range(NTILE)], stream="cy")

    def em_gbc(eng):
        eng.matmul(PP[0][:, 0:512], ONE1F[:, :], GAR[:, 0:512], start=True, stop=True)
        return eng.matmul(PP[0][:, 512:1024], ONE1F[:, :], GAR[:, 512:1024], start=True, stop=True)
    S.add("pe", em_gbc, r=["ONE1F", "GAR"], w=[BK[0], BK[1]])
    S.add("dve", lambda eng: eng.tensor_copy(GBC[:, :], PP[0][:, :]), r=[BK[0], BK[1]], w=["GBC"])
    S.add("dve", lambda eng: eng.tensor_mul(PBG[:, :], GAR[:, :], PBR[:, :]), r=["GAR", "PBR"], w=["PBG"])
    S.add("dve", lambda eng: eng.tensor_mul(C2BG[:, :], GAR[:, :], C2BR[:, :]), r=["GAR", "C2BR"], w=["C2BG"])
    ctxI.close()
    S.phase_barrier()

    def transpose_blocks(src_fn, src_keys, dst_fn, dst_keys, nblk, banks2, affine=None):
        for g in range(0, nblk, 4):
            hi = min(g + 4, nblk)
            bki = banks2[(g // 4) % len(banks2)]

            def em_tr(eng, g=g, hi=hi, bki=bki):
                last = None
                for c in range(g, hi):
                    last = eng.transpose(bank(bki)[:, (c - g) * 128:(c - g) * 128 + 128],
                                         src_fn(c), IDENT[:, :])
                return last
            S.add("pe", em_tr, r=list(src_keys) + ["IDENT"], w=[BK[bki]])

            if affine is None:
                def em_ev(eng, g=g, hi=hi, bki=bki):
                    last = None
                    for c in range(g, hi):
                        last = eng.tensor_copy(dst_fn(c), bank(bki)[:, (c - g) * 128:(c - g) * 128 + 128])
                    return last
                S.add("dve", em_ev, r=[BK[bki]], w=list(dst_keys))
            else:
                gfn, bfn, akeys, extra_fn = affine

                def em_ev(eng, g=g, hi=hi, bki=bki):
                    last = None
                    for c in range(g, hi):
                        src = bank(bki)[:, (c - g) * 128:(c - g) * 128 + 128]
                        last = eng.tensor_scalar(dst_fn(c), src, gfn(c), bfn(c),
                                                 ALU.mult, ALU.add)
                        if extra_fn is not None:
                            last = eng.tensor_scalar(extra_fn(c), src, gfn(c), bfn(c),
                                                     ALU.mult, ALU.add)
                    return last
                S.add("dve", em_ev, r=[BK[bki]] + list(akeys), w=list(dst_keys))

    ctxA = contextlib.ExitStack()
    sbA = lambda n, s, dt: ctxA.enter_context(nc.sbuf_tensor(n, list(s), dt))
    QKVT = sbA("QKVT", [128, 8, 3072], BF16)
    NSC = sbA("NSC", [128, D], F32)
    QSC = sbA("QSC", [128, D], F32)
    AFM = [sbA(f"AFM{i}", [128, 8, 128], BF16) for i in range(2)]
    KPH = sbA("KPH", [128, D], BF16)
    VX = sbA("VX", [128, 16, 65], BF16)
    STT = [sbA(f"STT{i}", [128, 8], F32) for i in range(2)]

    for c in range(8):
        dma("pool", QKVT[:, c, :], qkvT_d.rearrange("(c p) f -> c p f", p=128)[c],
            w=[f"QKVT{c}"], stream="p0")

        def em_fold(eng, c=c):
            return eng.tensor_scalar_mul(QKVT[:, c, :], QKVT[:, c, :], S1V[:, c:c + 1])
        S.add("dve", em_fold, r=[f"QKVT{c}", "S1V"], w=[f"QKVT{c}"])

    S.add("dve", lambda eng: eng.memset(VX[:, :, 64:65], 1.0), w=["VX"])
    S.add("dve", lambda eng: eng.memset(KVACC[:, :, :], 0.0), w=["KVACC"])

    for t in range(NTILE):
        i = t % 2
        st = STT[i]
        sk = f"STT{i}"

        def em_sq(eng, t=t, i=i):
            return eng.activation(NSC[:, :], X[:, t, :], AF.Square, bias=0.0,
                                  scale=1.0, accum_out=STT[i][:, 0:1])
        S.add("act", em_sq, r=[f"X{t}"], w=["NSC", sk])
        S.add("dve", lambda eng, t=t, i=i: eng.reduce_sum(STT[i][:, 1:2], X[:, t, :], axis=AX.X),
              r=[f"X{t}"], w=[sk])

        S.add("dve", lambda eng, st=st: eng.tensor_scalar_mul(st[:, 2:3], st[:, 1:2], 1.0 / 1024.0), r=[sk], w=[sk])
        S.add("dve", lambda eng, st=st: eng.tensor_mul(st[:, 3:4], st[:, 2:3], st[:, 1:2]), r=[sk], w=[sk])
        S.add("dve", lambda eng, st=st: eng.tensor_sub(st[:, 4:5], st[:, 0:1], st[:, 3:4]), r=[sk], w=[sk])
        S.add("dve", lambda eng, st=st: eng.reciprocal(st[:, 5:6], st[:, 4:5]), r=[sk], w=[sk])

        def em_f1(eng, st=st):
            return eng.activation(st[:, 3:4], st[:, 5:6], AF.Sqrt, bias=0.0, scale=CP[:, 1:2])
        S.add("act", em_f1, r=[sk, "CP"], w=[sk])

        S.add("dve", lambda eng, st=st: eng.tensor_scalar(st[:, 5:6], st[:, 4:5], 1.0 / 1024.0, EPS1, ALU.mult, ALU.add), r=[sk], w=[sk])
        S.add("dve", lambda eng, st=st: eng.reciprocal(st[:, 6:7], st[:, 5:6]), r=[sk], w=[sk])

        def em_rstd(eng, st=st):
            return eng.activation(st[:, 5:6], st[:, 6:7], AF.Sqrt, bias=0.0, scale=1.0)
        S.add("act", em_rstd, r=[sk], w=[sk])

        def em_xa(eng, t=t, st=st):
            return eng.tensor_scalar(XA[:, t, :], X[:, t, :], st[:, 2:3], st[:, 5:6],
                                     ALU.subtract, ALU.mult)
        S.add("dve", em_xa, r=[f"X{t}", sk], w=[f"XA{t}"])

        if _stdump is not None:
            def em_std(eng, t=t, st=st):
                return eng.tensor_copy(_stdump[0][:, t, :], st[:, :])
            S.add("dve", em_std, r=[sk], w=["STD"])

        def em_a1(eng, t=t, st=st):
            return eng.tensor_scalar(NSC[:, :], X[:, t, :], st[:, 2:3], st[:, 3:4],
                                     ALU.subtract, ALU.mult)
        S.add("dve", em_a1, r=[f"X{t}", sk, "NSC"], w=["NSC"])

        transpose_blocks(lambda c: NSC[:, c * 128:c * 128 + 128], ["NSC"],
                         lambda c, i=i: AFM[i][:, c, :], [f"AFM{i}"], 8, [6, 7])

        def em_qkv(eng, i=i):
            last = None
            for c in range(8):
                lhs = AFM[i][:, c, :]
                for ns in range(2):
                    last = eng.matmul(PP[1][:, ns * 512:ns * 512 + 512], lhs,
                                      QKVT[:, c, 1024 + ns * 512:1024 + ns * 512 + 512],
                                      start=(c == 0), stop=(c == 7))
                for ns in range(2):
                    last = eng.matmul(PP[2][:, ns * 512:ns * 512 + 512], lhs,
                                      QKVT[:, c, 2048 + ns * 512:2048 + ns * 512 + 512],
                                      start=(c == 0), stop=(c == 7))
                for ns in range(2):
                    last = eng.matmul(PP[0][:, ns * 512:ns * 512 + 512], lhs,
                                      QKVT[:, c, ns * 512:ns * 512 + 512],
                                      start=(c == 0), stop=(c == 7))
            return last
        S.add("pe", em_qkv, r=[f"AFM{i}"] + [f"QKVT{c}" for c in range(8)],
              w=[BK[0], BK[1], BK[2], BK[3], BK[4], BK[5]])

        def em_ksq(eng):
            return eng.activation(NSC[:, :], PP[1][:, :], AF.Square, bias=0.0, scale=1.0)
        S.add("act", em_ksq, r=[BK[2], BK[3], "NSC"], w=["NSC"])

        def em_kexp(eng):
            return eng.activation(KPH[:, :], NSC[:, :], AF.Exp, bias=0.0, scale=CP[:, 0:1])
        S.add("act", em_kexp, r=["NSC", "CP", "KPH"], w=["KPH"])

        def em_vev(eng):
            return eng.tensor_copy(VX[:, :, 0:64],
                                   PP[2][:, :].rearrange("p (h e) -> p h e", h=16))
        S.add("dve", em_vev, r=[BK[4], BK[5], "VX"], w=["VX"])

        def em_qsq(eng):
            return eng.activation(QSC[:, :], PP[0][:, :], AF.Square, bias=0.0, scale=1.0)
        S.add("act", em_qsq, r=[BK[0], BK[1], "QSC"], w=["QSC"])

        def em_qexp(eng):
            return eng.activation(QSC[:, :], QSC[:, :], AF.Exp, bias=0.0, scale=CP[:, 0:1])
        S.add("act", em_qexp, r=["QSC", "CP"], w=["QSC"])

        transpose_blocks(lambda c: QSC[:, c * 128:c * 128 + 128], ["QSC"],
                         lambda c, t=t: QPHI[:, c, t * 128:t * 128 + 128], ["QPHI"], 8, [0, 1])

        for c in range(8):
            bki = 2 + (c % 2)

            def em_kv(eng, c=c, bki=bki):
                last = None
                for hh in range(2):
                    h = 2 * c + hh
                    last = eng.matmul(bank(bki)[hh * 64:hh * 64 + 64, 0:65],
                                      KPH[:, h * 64:h * 64 + 64], VX[:, h, :],
                                      start=True, stop=True)
                return last
            S.add("pe", em_kv, r=["KPH", "VX"], w=[BK[bki]])

            def em_kvadd(eng, c=c, bki=bki):
                return eng.tensor_add(KVACC[:, c, :], KVACC[:, c, :], bank(bki)[:, 0:65])
            S.add("dve", em_kvadd, r=[BK[bki], "KVACC"], w=["KVACC"])

    if stop_phase <= 0:
        ctxA.close()
        ctxR2.close()
        ctxR1.close()
        S.phase_barrier()
        emit_output()
        S.emit_all(nc)
        return nc

    dma("sp", kv_ar_in.ap(), KVACC[:, :, :].rearrange("p c e -> p (c e)"),
        r=["KVACC"], w=["d_kvin"], stream="cl")

    def em_ar(eng):
        return eng.collective_compute(
            "AllReduce", ALU.add,
            replica_groups=[[0, 1, 2, 3], [4, 5, 6, 7]],
            ins=[kv_ar_in.ap().opt()],
            outs=[kv_ar_out.ap().opt()],
        )
    S.add("pool", em_ar, r=["d_kvin"], w=["d_kvout"], kind="cc")

    ctxA.close()
    S.phase_barrier()
    ctxA2 = contextlib.ExitStack()
    sbA2 = lambda n, s, dt: ctxA2.enter_context(nc.sbuf_tensor(n, list(s), dt))
    KVB = sbA2("KVB", [128, 8, 65], F32)
    KVBF = sbA2("KVBF", [128, 8, 65], BF16)
    KSM = sbA2("KSM", [128, 8, 2], BF16)
    PROJT = sbA2("PROJT", [128, 8, 1024], BF16)
    ZT = [sbA2(f"ZT{i}", [128, 16], F32) for i in range(3)]
    OSC = [sbA2(f"OSC{i}", [128, D], F32) for i in range(3)]
    OFM = [sbA2(f"OFM{i}", [128, 8, 128], BF16) for i in range(3)]

    dma("sp", KVB[:, :, :].rearrange("p c e -> p (c e)"), kv_ar_out.ap(),
        r=["d_kvout"], w=["KVB"], stream="cl")
    S.add("dve", lambda eng: eng.tensor_copy(KVBF[:, :, :], KVB[:, :, :]), r=["KVB"], w=["KVBF"])
    S.add("dve", lambda eng: eng.memset(KSM[:, :, :], 0.0), w=["KSM"])

    def em_ksm(eng):
        last = None
        for c in range(8):
            eng.tensor_copy(KSM[0:64, c, 0:1], KVB[0:64, c, 64:65])
            last = eng.tensor_copy(KSM[64:128, c, 1:2], KVB[64:128, c, 64:65])
        return last
    S.add("dve", em_ksm, r=["KVB", "KSM"], w=["KSM"])

    for c in range(8):
        dma("pool", PROJT[:, c, :], projT_d.rearrange("(c p) f -> c p f", p=128)[c],
            w=[f"PROJT{c}"], stream="p0")

        def em_pfold(eng, c=c):
            return eng.tensor_mul(PROJT[:, c, :], PROJT[:, c, :], GBC[:, :])
        S.add("dve", em_pfold, r=[f"PROJT{c}", "GBC"], w=[f"PROJT{c}"])

    if _os.environ.get("DEVTAP") == "A":
        dbg_xa = nc.dram_tensor("dbg_xa", [128, NTILE, D], BF16, kind="ExternalOutput").ap()
        dbg_qp = nc.dram_tensor("dbg_qp", [128, 8, T_CORE], BF16, kind="ExternalOutput").ap()
        dbg_kv = nc.dram_tensor("dbg_kv", [128, 8, 65], F32, kind="ExternalOutput").ap()
        dbg_kvb = nc.dram_tensor("dbg_kvb", [128, 8, 65], F32, kind="ExternalOutput").ap()
        dma("sp", dbg_xa, XA[:, :, :], r=[f"XA{t}" for t in range(NTILE)], stream="cy")
        dma("sp", dbg_qp, QPHI[:, :, :], r=["QPHI"], stream="cy")
        dma("sp", dbg_kv, KVACC[:, :, :], r=["KVACC"], stream="cy")
        dma("sp", dbg_kvb, KVB[:, :, :], r=["KVB"], stream="cy")
        if _stdump is not None:
            dma("sp", _stdump[1], _stdump[0][:, :, :], r=["STD"], stream="cy")

    if stop_phase <= 0.71:
        ctxA2.close()
        ctxR2.close()
        ctxR1.close()
        S.phase_barrier()
        emit_output()
        S.emit_all(nc)
        return nc

    _sub = int(_os.environ.get("PASSB_SUB", "9"))
    passb_tiles = NTILE if stop_phase > 0.9 else max(1, int(round((stop_phase - 0.7) * 40)))

    def passb_stage1(t):
        i = t % 3

        def em_den(eng, t=t):
            last = None
            for c in range(8):
                last = eng.matmul(bank(3)[:, 2 * c:2 * c + 2],
                                  QPHI[:, c, t * 128:t * 128 + 128], KSM[:, c, :],
                                  start=True, stop=True)
            return last
        S.add("pe", em_den, r=["QPHI", "KSM"], w=[BK[3]])

        def em_zadd(eng, i=i):
            return eng.activation(ZT[i][:, :], bank(3)[:, 0:16], AF.Identity,
                                  bias=CP[:, 2:3], scale=1.0)
        S.add("act", em_zadd, r=[BK[3], "CP"], w=[f"ZT{i}"])
        S.add("dve", lambda eng, i=i: eng.reciprocal(ZT[i][:, :], ZT[i][:, :]),
              r=[f"ZT{i}"], w=[f"ZT{i}"])
        if _sub <= 1:
            return

        for c in range(8):
            off = 1024 * (c % 2)
            bks = [BK[4 + 2 * (c % 2)], BK[5 + 2 * (c % 2)]]

            def em_out(eng, t=t, c=c, off=off):
                last = None
                for hh in range(2):
                    last = eng.matmul(PQ[1][:, off + hh * 512:off + hh * 512 + 64],
                                      QPHI[64 * hh:64 * hh + 64, c, t * 128:t * 128 + 128],
                                      KVBF[64 * hh:64 * hh + 64, c, 0:64],
                                      start=True, stop=True)
                return last
            S.add("pe", em_out, r=["QPHI", "KVBF"], w=bks)

            def em_osc(eng, i=i, c=c, off=off):
                pv = PQ[1][:, off:off + 1024].rearrange("p (h x) -> p h x", h=2)[:, :, 0:64]
                return eng.tensor_tensor(
                    OSC[i][:, c * 128:c * 128 + 128].rearrange("p (h e) -> p h e", h=2),
                    pv,
                    ZT[i][:, 2 * c:2 * c + 2].unsqueeze(2).broadcast_to([128, 2, 64]),
                    ALU.mult)
            S.add("dve", em_osc, r=bks + [f"ZT{i}", f"OSC{i}"], w=[f"OSC{i}"])

        if _sub <= 2:
            return
        transpose_blocks(lambda c, i=i: OSC[i][:, c * 128:c * 128 + 128], [f"OSC{i}"],
                         lambda c, i=i: OFM[i][:, c, :], [f"OFM{i}"], 8, [4, 5])

    def passb_stage2(t):
        i = t % 3

        def em_proj(eng, i=i):
            last = None
            for c in range(8):
                for ns in range(2):
                    last = eng.matmul(PP[0][:, ns * 512:ns * 512 + 512], OFM[i][:, c, :],
                                      PROJT[:, c, ns * 512:ns * 512 + 512],
                                      start=(c == 0), stop=False)
            for ns in range(2):
                last = eng.matmul(PP[0][:, ns * 512:ns * 512 + 512], ONE1[:, :],
                                  PBG[:, ns * 512:ns * 512 + 512], start=False, stop=True)
            return last
        S.add("pe", em_proj, r=[f"OFM{i}", "ONE1", "PBG"] + [f"PROJT{c}" for c in range(8)],
              w=[BK[0], BK[1]])

        def em_x1a(eng, t=t, i=i):
            eng.tensor_copy(OSC[i][:, :], XA[:, t, :])
            return eng.tensor_add(OSC[i][:, :], OSC[i][:, :], PP[0][:, :])
        S.add("dve", em_x1a, r=[BK[0], BK[1], f"XA{t}", f"OSC{i}"], w=[f"OSC{i}"])

        def em_x1b(eng, t=t, i=i):
            return eng.tensor_add(X[:, t, :], X[:, t, :], OSC[i][:, :])
        S.add("dve", em_x1b, r=[f"OSC{i}", f"X{t}"], w=[f"X{t}"])

        def em_x1c(eng, t=t, i=i):
            return eng.copy(XA[:, t, :], OSC[i][:, :])
        S.add("act", em_x1c, r=[f"OSC{i}"], w=[f"XA{t}"])

    passb_stage1(0)
    if passb_tiles > 1:
        passb_stage1(1)
    for t in range(passb_tiles):
        if t + 2 < passb_tiles:
            passb_stage1(t + 2)
        if _sub >= 4:
            passb_stage2(t)

    ctxA2.close()
    ctxR2.close()
    S.phase_barrier()
    if stop_phase <= 1:
        ctxR1.close()
        emit_output()
        S.emit_all(nc)
        return nc

    ctxC = contextlib.ExitStack()
    sbC = lambda n, s, dt: ctxC.enter_context(nc.sbuf_tensor(n, list(s), dt))
    CONV1T = sbC("CONV1T", [128, 8, 2048], BF16)
    CONV2T = sbC("CONV2T", [128, 16, 1024], BF16)
    C1B = sbC("C1B", [128, 16], F32)
    NSC2 = sbC("NSC2", [128, D], F32)
    AFM4 = sbC("AFM4", [128, 8, 512], BF16)
    H1 = sbC("H1", [128, 16, 512], BF16)
    ST2 = [sbC(f"ST2_{i}", [128, 4], F32) for i in range(2)]

    dma("sp", C1B[:, :], conv1b_d, w=["C1B"])
    for c in range(8):
        dma("pool", CONV1T[:, c, :], conv1T_d.rearrange("(c p) f -> c p f", p=128)[c],
            w=[f"CONV1T{c}"], stream="p0")

        def em_c1fold(eng, c=c):
            return eng.tensor_scalar_mul(CONV1T[:, c, :], CONV1T[:, c, :], S2V[:, c:c + 1])
        S.add("dve", em_c1fold, r=[f"CONV1T{c}", "S2V"], w=[f"CONV1T{c}"])
    for c in range(16):
        dma("pool", CONV2T[:, c, :], conv2T_d.rearrange("(c p) f -> c p f", p=128)[c],
            w=[f"CONV2T{c}"], stream="p0")

        def em_c2fold(eng, c=c):
            return eng.tensor_mul(CONV2T[:, c, :], CONV2T[:, c, :], GBC[:, :])
        S.add("dve", em_c2fold, r=[f"CONV2T{c}", "GBC"], w=[f"CONV2T{c}"])

    for q in range(4):
        iq = q % 2
        for tl in range(4):
            t = q * 4 + tl

            def em_sq2(eng, t=t, iq=iq, tl=tl):
                return eng.activation(NSC2[:, :], XA[:, t, :], AF.Square, bias=0.0,
                                      scale=1.0, accum_out=ST2[iq][:, tl:tl + 1])
            S.add("act", em_sq2, r=[f"XA{t}", "NSC2"], w=["NSC2", f"ST2_{iq}"])

            def em_rr(eng, iq=iq, tl=tl):
                return eng.reciprocal(ST2[iq][:, tl:tl + 1], ST2[iq][:, tl:tl + 1])
            S.add("dve", em_rr, r=[f"ST2_{iq}"], w=[f"ST2_{iq}"])

            def em_f2(eng, iq=iq, tl=tl):
                return eng.activation(ST2[iq][:, tl:tl + 1], ST2[iq][:, tl:tl + 1], AF.Sqrt,
                                      bias=0.0, scale=CP[:, 1:2])
            S.add("act", em_f2, r=[f"ST2_{iq}", "CP"], w=[f"ST2_{iq}"])

            def em_a2(eng, t=t, iq=iq, tl=tl):
                return eng.tensor_scalar_mul(NSC2[:, :], XA[:, t, :], ST2[iq][:, tl:tl + 1])
            S.add("dve", em_a2, r=[f"XA{t}", f"ST2_{iq}", "NSC2"], w=["NSC2"])

            transpose_blocks(lambda c: NSC2[:, c * 128:c * 128 + 128], ["NSC2"],
                             lambda c, tl=tl: AFM4[:, c, tl * 128:tl * 128 + 128],
                             ["AFM4"], 8, [0, 1])

        for fc in range(16):
            bki = fc % 4

            def em_c1(eng, fc=fc, bki=bki):
                last = None
                for c in range(8):
                    last = eng.matmul(bank(bki)[:, :], CONV1T[:, c, fc * 128:fc * 128 + 128],
                                      AFM4[:, c, :], start=(c == 0), stop=(c == 7))
                return last
            S.add("pe", em_c1, r=["AFM4"] + [f"CONV1T{c}" for c in range(8)], w=[BK[bki]])

            def em_gelu(eng, fc=fc, bki=bki):
                return eng.activation(H1[:, fc, :], bank(bki)[:, :], AF.Gelu,
                                      bias=C1B[:, fc:fc + 1], scale=1.0)
            S.add("act", em_gelu, r=[BK[bki], "C1B", "H1"], w=["H1"])

        for tl in range(4):
            t = q * 4 + tl
            pj = 2 + (tl % 2)

            def em_c2(eng, tl=tl, pj=pj):
                last = None
                for c in range(16):
                    for ns in range(2):
                        last = eng.matmul(PP[pj][:, ns * 512:ns * 512 + 512],
                                          H1[:, c, tl * 128:tl * 128 + 128],
                                          CONV2T[:, c, ns * 512:ns * 512 + 512],
                                          start=(c == 0), stop=False)
                for ns in range(2):
                    last = eng.matmul(PP[pj][:, ns * 512:ns * 512 + 512], ONE1[:, :],
                                      C2BG[:, ns * 512:ns * 512 + 512], start=False, stop=True)
                return last
            S.add("pe", em_c2, r=["H1", "ONE1", "C2BG"] + [f"CONV2T{c}" for c in range(16)],
                  w=PKEY[pj])

            def em_xup(eng, t=t, pj=pj):
                return eng.tensor_add(X[:, t, :], X[:, t, :], PP[pj][:, :])
            S.add("dve", em_xup, r=PKEY[pj] + [f"X{t}"], w=[f"X{t}"])

    ctxC.close()
    ctxR1.close()
    S.phase_barrier()
    if stop_phase <= 2:
        emit_output()
        S.emit_all(nc)
        return nc

    def ffn_call(callno):
        ctx = contextlib.ExitStack()
        pfx = f"f{callno}_"
        sbF = lambda n, s, dt: ctx.enter_context(nc.sbuf_tensor(pfx + n, list(s), dt))
        XN = sbF("XN", [128, 8, T_CORE], BF16)
        XNF = [sbF(pfx + "XNF0", [128, 8, 128], F32)] * 2
        NSCF = [sbF(f"NSCF{i}", [128, D], F32) for i in range(2)]
        STF = [sbF(f"STF{i}", [128, 8], F32) for i in range(2)]
        PGb = [sbF(f"PG{i}", [128, 8], F32) for i in range(2)]
        INDb = [sbF(f"IND{i}", [128, 8], F32) for i in range(2)]
        W1S = [sbF(f"W1S{i}", [128, 8, 512], BF16) for i in range(2)]
        W2S = [sbF(f"W2S{i}", [128, 1024], BF16) for i in range(2)]
        HFb = sbF("HF", [128, 32, 512], BF16)
        B2 = sbF("B2", [1, 8, 1024], BF16)

        dma("pool", B2[:, :, :], b2r_d.rearrange("r (e o) -> r e o", e=8), w=[pfx + "B2"], stream="p0")

        def ln_tile(t):
            i = t % 2
            st = STF[i]
            sk = pfx + f"STF{i}"

            def em_sq(eng, t=t, i=i):
                return eng.activation(NSCF[i][:, :], X[:, t, :], AF.Square, bias=0.0,
                                      scale=1.0, accum_out=STF[i][:, 0:1])
            S.add("act", em_sq, r=[f"X{t}", pfx + f"NSCF{i}"], w=[pfx + f"NSCF{i}", sk])
            S.add("dve", lambda eng, t=t, i=i: eng.reduce_sum(STF[i][:, 1:2], X[:, t, :], axis=AX.X),
                  r=[f"X{t}"], w=[sk])

            S.add("dve", lambda eng, st=st: eng.tensor_scalar_mul(st[:, 2:3], st[:, 1:2], 1.0 / 1024.0), r=[sk], w=[sk])
            S.add("dve", lambda eng, st=st: eng.tensor_mul(st[:, 3:4], st[:, 2:3], st[:, 1:2]), r=[sk], w=[sk])
            S.add("dve", lambda eng, st=st: eng.tensor_sub(st[:, 3:4], st[:, 0:1], st[:, 3:4]), r=[sk], w=[sk])
            S.add("dve", lambda eng, st=st: eng.tensor_scalar(st[:, 5:6], st[:, 3:4], (1.0 + EPS2) / 1024.0, EPS1 * EPS2, ALU.mult, ALU.add), r=[sk], w=[sk])
            S.add("dve", lambda eng, st=st: eng.reciprocal(st[:, 4:5], st[:, 5:6]), r=[sk], w=[sk])
            S.add("dve", lambda eng, st=st: eng.tensor_scalar(st[:, 5:6], st[:, 3:4], 1.0 / 1024.0, EPS1, ALU.mult, ALU.add), r=[sk], w=[sk])
            S.add("dve", lambda eng, st=st: eng.reciprocal(st[:, 7:8], st[:, 5:6]), r=[sk], w=[sk])

            def em_gs(eng, st=st):
                eng.activation(st[:, 4:5], st[:, 4:5], AF.Sqrt, bias=0.0, scale=1.0)
                return eng.activation(st[:, 7:8], st[:, 7:8], AF.Sqrt, bias=0.0, scale=1.0)
            S.add("act", em_gs, r=[sk], w=[sk])

            def em_ln(eng, t=t, i=i, st=st):
                return eng.tensor_scalar(NSCF[i][:, :], X[:, t, :], st[:, 2:3], st[:, 4:5],
                                         ALU.subtract, ALU.mult)
            S.add("dve", em_ln, r=[f"X{t}", sk, pfx + f"NSCF{i}"], w=[pfx + f"NSCF{i}"])

            transpose_blocks(lambda c, i=i: NSCF[i][:, c * 128:c * 128 + 128], [pfx + f"NSCF{i}"],
                             lambda c, t=t: XN[:, c, t * 128:t * 128 + 128],
                             [pfx + "XN", pfx + "XNF0"], 8, [0, 1],
                             affine=(lambda c: GF[:, c:c + 1], lambda c: BFv[:, c:c + 1],
                                     ["GF", "BFv"],
                                     lambda c, i=i: XNF[i][:, c, :]))

            def em_lnres(eng, t=t, i=i, st=st):
                return eng.tensor_scalar(NSCF[i][:, :], X[:, t, :], st[:, 2:3], st[:, 7:8],
                                         ALU.subtract, ALU.mult)
            S.add("dve", em_lnres, r=[f"X{t}", sk, pfx + f"NSCF{i}"], w=[pfx + f"NSCF{i}"])

            def em_lnadd(eng, t=t, i=i):
                return eng.tensor_add(X[:, t, :], X[:, t, :], NSCF[i][:, :])
            S.add("dve", em_lnadd, r=[f"X{t}", pfx + f"NSCF{i}"], w=[f"X{t}"])

            def em_gl(eng, i=i):
                last = None
                for c in range(8):
                    last = eng.matmul(bank(7)[:, 0:8], XNF[i][:, c, :], WG[:, c, :],
                                      start=(c == 0), stop=False)
                return eng.matmul(bank(7)[:, 0:8], ONE1F[:, :], BGR[:, :],
                                  start=False, stop=True)
            S.add("pe", em_gl, r=[pfx + "XNF0", "WG", "ONE1F", "BGR"], w=[BK[7]])

            def em_nm(eng, st=st):
                return eng.reduce_max(st[:, 5:6], bank(7)[:, 0:8], axis=AX.X, negate=True)
            S.add("dve", em_nm, r=[BK[7], sk], w=[sk])

            def em_exp(eng, i=i, st=st):
                return eng.activation(PGb[i][:, :], bank(7)[:, 0:8], AF.Exp,
                                      bias=st[:, 5:6], scale=1.0, accum_out=st[:, 6:7])
            S.add("act", em_exp, r=[BK[7], sk, pfx + f"PG{i}"], w=[pfx + f"PG{i}", sk])

            P = PGb[i][:, :]
            IN = INDb[i][:, :]
            pk = pfx + f"PG{i}"
            ik = pfx + f"IND{i}"
            S.add("dve", lambda eng, P=P, st=st: eng.reduce_max(st[:, 0:1], P, axis=AX.X), r=[pk, sk], w=[sk])
            S.add("dve", lambda eng, P=P, IN=IN, st=st: eng.tensor_single_scalar(IN, P, st[:, 0:1], ALU.is_ge), r=[pk, sk, ik], w=[ik])
            S.add("dve", lambda eng, P=P, IN=IN: eng.scalar_tensor_tensor(IN, IN, NEG, P, ALU.mult, ALU.add), r=[pk, ik], w=[ik])
            S.add("dve", lambda eng, IN=IN, st=st: eng.reduce_max(st[:, 1:2], IN, axis=AX.X), r=[ik, sk], w=[sk])
            S.add("dve", lambda eng, P=P, IN=IN, st=st: eng.tensor_single_scalar(IN, P, st[:, 1:2], ALU.is_ge), r=[pk, sk, ik], w=[ik])
            S.add("dve", lambda eng, st=st: eng.tensor_add(st[:, 2:3], st[:, 0:1], st[:, 1:2]), r=[sk], w=[sk])
            S.add("dve", lambda eng, st=st: eng.scalar_tensor_tensor(st[:, 2:3], st[:, 6:7], 1e-8, st[:, 2:3], ALU.mult, ALU.add), r=[sk], w=[sk])
            S.add("dve", lambda eng, st=st: eng.reciprocal(st[:, 2:3], st[:, 2:3]), r=[sk], w=[sk])
            S.add("dve", lambda eng, P=P, IN=IN, st=st, t=t: eng.scalar_tensor_tensor(WTM[:, t, :], P, st[:, 2:3], IN, ALU.mult, ALU.mult), r=[pk, sk, ik], w=["WTM"])

        for t in range(4):
            ln_tile(t)
        w1c = [0]
        w2c = [0]
        first = (callno == 1)
        for tb in range(4):
            for e in range(8):
                for fs in range(8):
                    j = w1c[0] % 2
                    w1c[0] += 1
                    wkey = pfx + f"W1S{j}"
                    if first and tb == 0:
                        dma("pool", W1S[j][:, :, :], w1_s[e, fs].rearrange("c p x -> p c x"),
                            w=[wkey], stream=f"w1{j}")
                        dma("sp", w1_st.ap()[e, fs].rearrange("c p x -> p c x"),
                            W1S[j][:, :, :], r=[wkey], w=[f"stw1_{e}_{fs}"], stream="sw1")
                    else:
                        dma("sp", W1S[j][:, :, :],
                            w1_st.ap()[e, fs].rearrange("c p x -> p c x"),
                            r=[f"stw1_{e}_{fs}"], w=[wkey], stream=f"w1{j}")
                    for fcl in range(4):
                        fc = fs * 4 + fcl
                        bki = fc % 4

                        def em_h(eng, j=j, fcl=fcl, tb=tb, bki=bki):
                            last = None
                            for c in range(8):
                                last = eng.matmul(bank(bki)[:, :],
                                                  W1S[j][:, c, fcl * 128:fcl * 128 + 128],
                                                  XN[:, c, tb * 512:tb * 512 + 512],
                                                  start=(c == 0), stop=(c == 7))
                            return last
                        S.add("pe", em_h, r=[wkey, pfx + "XN"], w=[BK[bki]])

                        def em_gel(eng, e=e, fc=fc, bki=bki):
                            return eng.activation(HFb[:, fc, :], bank(bki)[:, :], AF.Gelu,
                                                  bias=B1[:, e, fc:fc + 1], scale=1.0)
                        S.add("act", em_gel, r=[BK[bki], "B1"], w=[pfx + f"HF{fc}"])

                if e == 0 and tb + 1 < 4:
                    for t in range(4 * (tb + 1), 4 * (tb + 1) + 4):
                        ln_tile(t)
                for fc in range(32):
                    j2 = w2c[0] % 2
                    w2c[0] += 1
                    w2key = pfx + f"W2S{j2}"
                    if first and tb == 0:
                        dma("pool", W2S[j2][:, :], w2_s[e, fc], w=[w2key], stream=f"w2{j2}")
                        dma("sp", w2_st.ap()[e, fc], W2S[j2][:, :], r=[w2key],
                            w=[f"stw2_{e}_{fc}"], stream="sw2")
                    else:
                        dma("sp", W2S[j2][:, :], w2_st.ap()[e, fc],
                            r=[f"stw2_{e}_{fc}"], w=[w2key], stream=f"w2{j2}")

                    def em_y(eng, j2=j2, fc=fc):
                        last = None
                        for tl in range(4):
                            for ns in range(2):
                                last = eng.matmul(PP[tl][:, ns * 512:ns * 512 + 512],
                                                  HFb[:, fc, tl * 128:tl * 128 + 128],
                                                  W2S[j2][:, ns * 512:ns * 512 + 512],
                                                  start=(fc == 0), stop=False)
                        return last
                    S.add("pe", em_y, r=[w2key, pfx + f"HF{fc}"], w=ALLBK)

                def em_yb(eng, e=e):
                    last = None
                    for tl in range(4):
                        for ns in range(2):
                            last = eng.matmul(PP[tl][:, ns * 512:ns * 512 + 512], ONE1[:, :],
                                              B2[:, e, ns * 512:ns * 512 + 512],
                                              start=False, stop=True)
                    return last
                S.add("pe", em_yb, r=["ONE1", pfx + "B2"], w=ALLBK)

                for tl in range(4):
                    t = tb * 4 + tl

                    def em_comb(eng, t=t, tl=tl, e=e):
                        return eng.scalar_tensor_tensor(X[:, t, :], PP[tl][:, :],
                                                        WTM[:, t, e:e + 1], X[:, t, :],
                                                        ALU.mult, ALU.add)
                    S.add("dve", em_comb, r=PKEY[tl] + ["WTM", f"X{t}"], w=[f"X{t}"])

        ctx.close()
        S.phase_barrier()

    ctxPB = contextlib.ExitStack()
    sbPB = lambda n, s, dt: ctxPB.enter_context(nc.sbuf_tensor(n, list(s), dt))
    WKT = sbPB("WKT", [128, 8, 512], BF16)
    WVT = sbPB("WVT", [128, 8, 256], BF16)
    for c in range(8):
        dma("pool", WKT[:, c, :], WkT_d.rearrange("(c p) f -> c p f", p=128)[c], w=["WKT"], stream="p0")
        dma("pool", WVT[:, c, :], WvT_d.rearrange("(c p) f -> c p f", p=128)[c], w=["WVT"], stream="p0")

    ffn_call(1)
    if stop_phase <= 3:
        emit_output()
        S.emit_all(nc)
        return nc

    dma("sp", ag_in.ap(), X[:, 15, :], r=["X15"], w=["d_agin"], stream="cl")

    def em_ag(eng):
        return eng.collective_compute(
            "AllGather", ALU.bypass,
            replica_groups=[list(range(8))],
            ins=[ag_in.ap().opt()],
            outs=[ag_out.ap().opt()],
        )
    S.add("pool", em_ag, r=["d_agin"], w=["d_agout"], kind="cc")

    ctxR4 = contextlib.ExitStack()
    sbR4 = lambda n, s, dt: ctxR4.enter_context(nc.sbuf_tensor(n, list(s), dt, side="right"))
    QFM = sbR4("QFM", [128, 8, 2048], BF16)
    KFM = sbR4("KFM", [128, 4, 2176], BF16)
    VTM = sbR4("VTM", [128, 17, 256], BF16)
    MASKS = sbR4("MASKS", [128, 2, 256], F32)

    ctxB = contextlib.ExitStack()
    sbB = lambda n, s, dt: ctxB.enter_context(nc.sbuf_tensor(n, list(s), dt))
    AFMB = [sbB(f"AFMB{i}", [128, 8, 128], BF16) for i in range(2)]
    WQT = sbB("WQT", [128, 8, 1024], BF16)
    ROPEC = sbB("ROPEC", [128, 17, 32], BF16)
    ROPES = sbB("ROPES", [128, 17, 32], BF16)
    NSCB = sbB("NSCB", [128, D], F32)
    QSCB = sbB("QSCB", [128, D], F32)
    KSCB = sbB("KSCB", [128, 512], F32)
    TMPR = sbB("TMPR", [128, 1024], F32)
    STB = sbB("STB", [128, 8], F32)
    HALO = sbB("HALO", [128, D], F32)
    SLC = QSCB

    dma("sp", MASKS[:, :, :], masks_d, w=["MASKS"])
    dma("pool", ROPEC[:, :, :], ropeC_d.rearrange("t p f -> p t f"), w=["ROPEC"], stream="p0")
    dma("pool", ROPES[:, :, :], ropeS_d.rearrange("t p f -> p t f"), w=["ROPES"], stream="p0")
    for c in range(8):
        dma("pool", WQT[:, c, :], WqT_d.rearrange("(c p) f -> c p f", p=128)[c], w=[f"WQT{c}"], stream="p0")

        def em_qfold(eng, c=c):
            return eng.tensor_scalar_mul(WQT[:, c, :], WQT[:, c, :], 0.125)
        S.add("dve", em_qfold, r=[f"WQT{c}"], w=[f"WQT{c}"])

    for r8 in range(8):
        dma("sp", SLC[:, :], ag_out.ap()[r8 * 128:r8 * 128 + 128, :],
            r=["d_agout", "QSCB"], w=["QSCB"], stream="cl")
        if r8 == 0:
            def em_h0(eng):
                return eng.tensor_scalar_mul(HALO[:, :], SLC[:, :], HSEL[:, 0:1])
            S.add("dve", em_h0, r=["QSCB", "HSEL", "HALO"], w=["HALO"])
        else:
            def em_hr(eng, r8=r8):
                return eng.scalar_tensor_tensor(HALO[:, :], SLC[:, :], HSEL[:, r8:r8 + 1],
                                                HALO[:, :], ALU.mult, ALU.add)
            S.add("dve", em_hr, r=["QSCB", "HSEL", "HALO"], w=["HALO"])

    def b_rope(psum_keys, psum_view, nh, tt, dst, dst_key):
        cosb = ROPEC[:, tt, :].unsqueeze(1).broadcast_to([128, nh, 32])
        sinb = ROPES[:, tt, :].unsqueeze(1).broadcast_to([128, nh, 32])
        x1 = psum_view[:, :, 0:32]
        x2 = psum_view[:, :, 32:64]
        ta = TMPR[:, 0:nh * 32].rearrange("p (h f) -> p h f", h=nh)
        tbv = TMPR[:, nh * 32:nh * 64].rearrange("p (h f) -> p h f", h=nh)
        dv = dst.rearrange("p (h f) -> p h f", h=nh)

        def em(eng):
            eng.tensor_tensor(ta, x1, cosb, ALU.mult)
            eng.tensor_tensor(tbv, x2, sinb, ALU.mult)
            eng.tensor_sub(dv[:, :, 0:32], ta, tbv)
            eng.tensor_tensor(ta, x1, sinb, ALU.mult)
            eng.tensor_tensor(tbv, x2, cosb, ALU.mult)
            return eng.tensor_add(dv[:, :, 32:64], ta, tbv)
        S.add("dve", em, r=list(psum_keys) + ["ROPEC", "ROPES", "TMPR", dst_key],
              w=[dst_key, "TMPR"])

    for tt in list(range(1, 17)) + [0]:
        src = HALO[:, :] if tt == 0 else X[:, tt - 1, :]
        srck = "HALO" if tt == 0 else f"X{tt - 1}"
        ib = tt % 2

        def em_sq(eng, src=src):
            return eng.activation(NSCB[:, :], src, AF.Square, bias=0.0,
                                  scale=1.0, accum_out=STB[:, 0:1])
        S.add("act", em_sq, r=[srck, "NSCB"], w=["NSCB", "STB"])
        S.add("dve", lambda eng, src=src: eng.reduce_sum(STB[:, 1:2], src, axis=AX.X),
              r=[srck, "STB"], w=["STB"])

        S.add("dve", lambda eng: eng.tensor_scalar_mul(STB[:, 2:3], STB[:, 1:2], 1.0 / 1024.0), r=["STB"], w=["STB"])
        S.add("dve", lambda eng: eng.tensor_mul(STB[:, 3:4], STB[:, 2:3], STB[:, 1:2]), r=["STB"], w=["STB"])
        S.add("dve", lambda eng: eng.tensor_sub(STB[:, 3:4], STB[:, 0:1], STB[:, 3:4]), r=["STB"], w=["STB"])
        S.add("dve", lambda eng: eng.tensor_scalar(STB[:, 3:4], STB[:, 3:4], 1.0 / 1024.0, EPS1, ALU.mult, ALU.add), r=["STB"], w=["STB"])
        S.add("dve", lambda eng: eng.reciprocal(STB[:, 4:5], STB[:, 3:4]), r=["STB"], w=["STB"])
        S.add("act", lambda eng: eng.activation(STB[:, 4:5], STB[:, 4:5], AF.Sqrt,
                                                bias=0.0, scale=1.0), r=["STB"], w=["STB"])

        def em_ln(eng, src=src):
            return eng.tensor_scalar(NSCB[:, :], src, STB[:, 2:3], STB[:, 4:5],
                                     ALU.subtract, ALU.mult)
        S.add("dve", em_ln, r=[srck, "STB", "NSCB"], w=["NSCB"])

        transpose_blocks(lambda c: NSCB[:, c * 128:c * 128 + 128], ["NSCB"],
                         lambda c, ib=ib: AFMB[ib][:, c, :], [f"AFMB{ib}"], 8, [0, 1])

        def em_kmm(eng, ib=ib):
            last = None
            for c in range(8):
                last = eng.matmul(bank(4)[:, 0:512], AFMB[ib][:, c, :],
                                  WKT[:, c, :], start=(c == 0), stop=(c == 7))
            return last
        S.add("pe", em_kmm, r=[f"AFMB{ib}", "WKT"], w=[BK[4]])
        b_rope([BK[4]], bank(4)[:, 0:512].rearrange("p (h f) -> p h f", h=8), 8, tt,
               KSCB[:, :], "KSCB")
        transpose_blocks(lambda c: KSCB[:, c * 128:c * 128 + 128], ["KSCB"],
                         lambda c, tt=tt: KFM[:, c, tt * 128:tt * 128 + 128], ["KFM"], 4, [6])

        def em_vmm(eng, ib=ib):
            last = None
            for c in range(8):
                last = eng.matmul(bank(5)[:, 0:256], AFMB[ib][:, c, :],
                                  WVT[:, c, :], start=(c == 0), stop=(c == 7))
            return last
        S.add("pe", em_vmm, r=[f"AFMB{ib}", "WVT"], w=[BK[5]])

        def em_vev(eng, tt=tt):
            return eng.tensor_copy(VTM[:, tt, :], bank(5)[:, 0:256])
        S.add("dve", em_vev, r=[BK[5], "VTM"], w=["VTM"])

        if tt >= 1:
            t = tt - 1

            def em_qmm(eng, ib=ib):
                last = None
                for c in range(8):
                    for ns in range(2):
                        last = eng.matmul(PP[1][:, ns * 512:ns * 512 + 512],
                                          AFMB[ib][:, c, :],
                                          WQT[:, c, ns * 512:ns * 512 + 512],
                                          start=(c == 0), stop=(c == 7))
                return last
            S.add("pe", em_qmm, r=[f"AFMB{ib}"] + [f"WQT{c}" for c in range(8)],
                  w=[BK[2], BK[3]])
            b_rope([BK[2], BK[3]], PP[1][:, :].rearrange("p (h f) -> p h f", h=16), 16, tt,
                   QSCB[:, :], "QSCB")
            transpose_blocks(lambda c: QSCB[:, c * 128:c * 128 + 128], ["QSCB"],
                             lambda c, t=t: QFM[:, c, t * 128:t * 128 + 128], ["QFM"], 8, [6, 7])

    ctxB.close()
    ctxPB.close()
    S.phase_barrier()

    ctxR3 = contextlib.ExitStack()
    OFMB = ctxR3.enter_context(nc.sbuf_tensor("OFMB", [128, 8, 2048], BF16))
    ctxB2 = contextlib.ExitStack()
    sbB2 = lambda n, s, dt: ctxB2.enter_context(nc.sbuf_tensor(n, list(s), dt))
    SSB = [sbB2(f"SSB{i}", [128, 4, 256], F32) for i in range(2)]
    PFB = [sbB2(f"PFB{i}", [128, 4, 256], F32) for i in range(2)]
    PTS = [sbB2(f"PTS{i}", [128, 8, 128], BF16) for i in range(2)]
    SMB = [sbB2(f"SMB{i}", [128, 8], F32) for i in range(2)]

    def sc_stage1(g):
        t, hg = divmod(g, 4)
        msel = 0 if t == 0 else 1
        i = g % 2
        par = g % 2
        PB = PQ[par]
        bkeys = [BK[4 * par + k] for k in range(4)]

        def em_sc(eng, t=t, hg=hg, PB=PB):
            last = None
            for j in range(4):
                h = hg * 4 + j
                hkv = h // 4
                last = eng.matmul(
                    PB[:, j * 512:j * 512 + 256],
                    QFM[64 * (h % 2):64 * (h % 2) + 64, h // 2, t * 128:t * 128 + 128],
                    KFM[64 * (h % 2):64 * (h % 2) + 64, hkv, t * 128:t * 128 + 256],
                    start=True, stop=True)
            return last
        S.add("pe", em_sc, r=["QFM", "KFM"], w=bkeys)

        def em_mask(eng, i=i, msel=msel, PB=PB):
            pv = PB[:, :].rearrange("p (j x) -> p j x", j=4)[:, :, 0:256]
            return eng.tensor_tensor(SSB[i][:, :, :], pv,
                                     MASKS[:, msel, :].unsqueeze(1).broadcast_to([128, 4, 256]),
                                     ALU.add)
        S.add("dve", em_mask, r=bkeys + ["MASKS", f"SSB{i}"], w=[f"SSB{i}"])

        def em_nm(eng, i=i):
            return eng.tensor_reduce(SMB[i][:, 0:4], SSB[i][:, :, :], axis=AX.X,
                                     op=ALU.max, negate=True)
        S.add("dve", em_nm, r=[f"SSB{i}", f"SMB{i}"], w=[f"SMB{i}"])

        def em_sub(eng, i=i):
            return eng.tensor_tensor(SSB[i][:, :, :], SSB[i][:, :, :],
                                     SMB[i][:, 0:4].unsqueeze(2).broadcast_to([128, 4, 256]),
                                     ALU.add)
        S.add("dve", em_sub, r=[f"SSB{i}", f"SMB{i}"], w=[f"SSB{i}"])

        def em_exp(eng, i=i):
            return eng.activation(SSB[i][:, :, :].rearrange("p j k -> p (j k)"),
                                  SSB[i][:, :, :].rearrange("p j k -> p (j k)"),
                                  AF.Exp, bias=0.0, scale=1.0)
        S.add("act", em_exp, r=[f"SSB{i}"], w=[f"SSB{i}"])

        def em_sm(eng, i=i):
            return eng.tensor_reduce(SMB[i][:, 4:8], SSB[i][:, :, :], axis=AX.X, op=ALU.add)
        S.add("dve", em_sm, r=[f"SSB{i}", f"SMB{i}"], w=[f"SMB{i}"])
        S.add("dve", lambda eng, i=i: eng.reciprocal(SMB[i][:, 4:8], SMB[i][:, 4:8]),
              r=[f"SMB{i}"], w=[f"SMB{i}"])

        def em_pn(eng, i=i):
            return eng.tensor_tensor(PFB[i][:, :, :], SSB[i][:, :, :],
                                     SMB[i][:, 4:8].unsqueeze(2).broadcast_to([128, 4, 256]),
                                     ALU.mult)
        S.add("dve", em_pn, r=[f"SSB{i}", f"SMB{i}", f"PFB{i}"], w=[f"PFB{i}"])

    def sc_stage2(g):
        t, hg = divmod(g, 4)
        i = g % 2
        par = g % 2
        PB = PQ[par]
        avb = [BK[4 * par], BK[4 * par + 1]]
        trb = [2, 3] if par == 0 else [6, 7]

        transpose_blocks(
            lambda c, i=i: PFB[i][:, :, :].rearrange("p j k -> p (j k)")[:, c * 128:c * 128 + 128],
            [f"PFB{i}"],
            lambda c, i=i: PTS[i][:, c, :], [f"PTS{i}"], 8, trb)

        def em_av(eng, t=t, hg=hg, i=i, PB=PB):
            last = None
            for j in range(4):
                hkv = (hg * 4 + j) // 4
                for kb in range(2):
                    last = eng.matmul(
                        PB[64 * (j % 2):64 * (j % 2) + 64, (j // 2) * 512:(j // 2) * 512 + 128],
                        VTM[:, t + kb, hkv * 64:hkv * 64 + 64],
                        PTS[i][:, 2 * j + kb, :],
                        start=(kb == 0), stop=(kb == 1))
            return last
        S.add("pe", em_av, r=["VTM", f"PTS{i}"], w=avb)

        def em_oev(eng, t=t, hg=hg, PB=PB):
            eng.tensor_copy(OFMB[:, hg * 2, t * 128:t * 128 + 128], PB[:, 0:128])
            return eng.tensor_copy(OFMB[:, hg * 2 + 1, t * 128:t * 128 + 128], PB[:, 512:640])
        S.add("dve", em_oev, r=avb + ["OFMB"], w=["OFMB"])

    sc_stage1(0)
    for g in range(64):
        if g + 1 < 64:
            sc_stage1(g + 1)
        sc_stage2(g)

    ctxB2.close()
    ctxR4.close()
    S.phase_barrier()

    ctxB3 = contextlib.ExitStack()
    sbB3 = lambda n, s, dt: ctxB3.enter_context(nc.sbuf_tensor(n, list(s), dt))
    W1B = [sbB3(f"W1B{i}", [128, 8, 512], BF16) for i in range(2)]
    W2B = [sbB3(f"W2B{i}", [128, 1024], BF16) for i in range(2)]
    HB = sbB3("HB", [128, 32, 512], BF16)

    w1bc = [0]
    w2bc = [0]
    for tb in range(4):
        for fs in range(8):
            j = w1bc[0] % 2
            w1bc[0] += 1
            wkey = f"W1B{j}"
            if tb == 0:
                dma("pool", W1B[j][:, :, :], w1b_s[fs].rearrange("c p x -> p c x"),
                    w=[wkey], stream=f"w1{j}")
                dma("sp", w1b_st.ap()[fs].rearrange("c p x -> p c x"), W1B[j][:, :, :],
                    r=[wkey], w=[f"stw1b_{fs}"], stream="sw1")
            else:
                dma("sp", W1B[j][:, :, :], w1b_st.ap()[fs].rearrange("c p x -> p c x"),
                    r=[f"stw1b_{fs}"], w=[wkey], stream=f"w1{j}")
            for fcl in range(4):
                fc = fs * 4 + fcl
                bki = fc % 4

                def em_h(eng, j=j, fcl=fcl, tb=tb, bki=bki):
                    last = None
                    for c in range(8):
                        last = eng.matmul(bank(bki)[:, :],
                                          W1B[j][:, c, fcl * 128:fcl * 128 + 128],
                                          OFMB[:, c, tb * 512:tb * 512 + 512],
                                          start=(c == 0), stop=(c == 7))
                    return last
                S.add("pe", em_h, r=[wkey, "OFMB"], w=[BK[bki]])

                def em_sil(eng, fc=fc, bki=bki):
                    return eng.activation(HB[:, fc, :], bank(bki)[:, :], AF.Silu,
                                          bias=0.0, scale=1.0)
                S.add("act", em_sil, r=[BK[bki], "HB"], w=["HB"])

        for fc in range(32):
            j2 = w2bc[0] % 2
            w2bc[0] += 1
            w2key = f"W2B{j2}"
            if tb == 0:
                dma("pool", W2B[j2][:, :], w2b_s[fc], w=[w2key], stream=f"w2{j2}")
                dma("sp", w2b_st.ap()[fc], W2B[j2][:, :], r=[w2key], w=[f"stw2b_{fc}"],
                    stream="sw2")
            else:
                dma("sp", W2B[j2][:, :], w2b_st.ap()[fc], r=[f"stw2b_{fc}"], w=[w2key],
                    stream=f"w2{j2}")

            def em_y(eng, j2=j2, fc=fc):
                last = None
                for tl in range(4):
                    for ns in range(2):
                        last = eng.matmul(PP[tl][:, ns * 512:ns * 512 + 512],
                                          HB[:, fc, tl * 128:tl * 128 + 128],
                                          W2B[j2][:, ns * 512:ns * 512 + 512],
                                          start=(fc == 0), stop=(fc == 31))
                return last
            S.add("pe", em_y, r=[w2key, "HB"], w=ALLBK)

        for tl in range(4):
            t = tb * 4 + tl

            def em_xup(eng, t=t, tl=tl):
                return eng.tensor_add(X[:, t, :], X[:, t, :], PP[tl][:, :])
            S.add("dve", em_xup, r=PKEY[tl] + [f"X{t}"], w=[f"X{t}"])

    ctxB3.close()
    ctxR3.close()
    S.phase_barrier()
    if stop_phase <= 4:
        emit_output()
        S.emit_all(nc)
        return nc

    ffn_call(2)

    emit_output()
    S.emit_all(nc)
    return nc


def _host_prep(inputs):
    f = np.ascontiguousarray
    x = np.asarray(inputs["L0_attn_a_x"], dtype=np.float32)

    qkvT = f(np.asarray(inputs["L0_attn_a_qkv_w"], np.float32).T)
    projT = f(np.asarray(inputs["L0_attn_a_proj_w"], np.float32).T)
    conv1T = f(np.asarray(inputs["L0_attn_a_conv1_w"], np.float32).T)
    conv2T = f(np.asarray(inputs["L0_attn_a_conv2_w"], np.float32).T)
    scale1v = f(np.asarray(inputs["L0_attn_a_scale1"], np.float32).reshape(8, 128).T)
    scale2v = f(np.asarray(inputs["L0_attn_a_scale2"], np.float32).reshape(8, 128).T)
    gammaA = np.asarray(inputs["L0_attn_a_gamma"], np.float32).reshape(1, D)
    projb = np.asarray(inputs["L0_attn_a_proj_b"], np.float32).reshape(1, D)
    conv1b = f(np.asarray(inputs["L0_attn_a_conv1_b"], np.float32).reshape(16, 128).T)
    conv2b = np.asarray(inputs["L0_attn_a_conv2_b"], np.float32).reshape(1, D)
    WqT = f(np.asarray(inputs["L0_attn_b_Wq"], np.float32).T)
    WkT = np.asarray(inputs["L0_attn_b_Wk"], np.float32).T
    WkT = f(np.repeat(WkT.reshape(1024, 4, 64), 2, axis=1).reshape(1024, 512))
    WvT = f(np.asarray(inputs["L0_attn_b_Wv"], np.float32).T)
    w1b = np.asarray(inputs["L0_attn_b_W1"], np.float32)
    w2b = np.asarray(inputs["L0_attn_b_W2"], np.float32)
    w1b_s = f(w1b.reshape(8, 512, 8, 128).transpose(0, 2, 3, 1))
    w2b_s = f(w2b.reshape(1024, 32, 128).transpose(1, 2, 0))
    gammaF = f(np.asarray(inputs["L0_ffn_gamma"], np.float32).reshape(8, 128).T)
    betaF = f(np.asarray(inputs["L0_ffn_beta"], np.float32).reshape(8, 128).T)
    WgT = f(np.asarray(inputs["L0_ffn_Wg"], np.float32).T)
    bg = np.asarray(inputs["L0_ffn_bg"], np.float32).reshape(1, 8)
    b1 = np.asarray(inputs["L0_ffn_b1"], np.float32)
    b1v = f(b1.reshape(8, 32, 128).transpose(2, 0, 1))
    b2r = f(np.asarray(inputs["L0_ffn_b2"], np.float32).reshape(1, 8192))
    w1 = np.asarray(inputs["L0_ffn_W1"], np.float32)
    w2 = np.asarray(inputs["L0_ffn_W2"], np.float32)
    w1_s = f(w1.reshape(8, 8, 512, 8, 128).transpose(0, 1, 3, 4, 2))
    w2_s = f(w2.transpose(0, 2, 1).reshape(8, 32, 128, 1024))

    identf = np.eye(128, dtype=np.float32)
    ones128 = np.ones((1, 128), dtype=np.float32)
    cpool = np.zeros((128, 8), dtype=np.float32)
    cpool[:, 0] = -0.5
    cpool[:, 1] = C_SQ
    cpool[:, 2] = 1e-6

    r_i = np.arange(128)[:, None]
    c_i = np.arange(256)[None, :]
    band = (c_i >= r_i + 1) & (c_i <= r_i + 128)
    mstd = np.where(band, 0.0, NEG).astype(np.float32)
    m0 = np.where(band & (c_i >= 128), 0.0, NEG).astype(np.float32)
    masks_first = f(np.stack([m0, mstd], axis=1))
    masks_mid = f(np.stack([mstd, mstd], axis=1))

    inv_freq = (1.0 / (10000.0 ** (np.arange(0, 64, 2, dtype=np.float32) / 64.0)))

    shared = dict(qkvT=qkvT, projT=projT, conv1T=conv1T, conv2T=conv2T,
                  scale1v=scale1v, scale2v=scale2v, gammaA=gammaA, projb=projb,
                  conv1b=conv1b, conv2b=conv2b, WqT=WqT, WkT=WkT, WvT=WvT,
                  w1b_s=w1b_s, w2b_s=w2b_s, gammaF=gammaF, betaF=betaF,
                  WgT=WgT, bg=bg, b1v=b1v, b2r=b2r, w1_s=w1_s, w2_s=w2_s,
                  identf=identf, ones128=ones128, cpool=cpool)

    in_maps = []
    for core in range(8):
        b = core // 4
        s0 = (core % 4) * T_CORE
        m = dict(shared)
        m["x"] = f(x[b, s0:s0 + T_CORE, :])
        m["masks"] = masks_first if core % 4 == 0 else masks_mid
        pos = np.maximum(np.arange(s0 - 128, s0 + T_CORE, dtype=np.float32), 0.0)
        freqs = pos[:, None] * inv_freq[None, :]
        m["ropeC"] = f(np.cos(freqs).reshape(17, 128, 32)).astype(np.float32)
        m["ropeS"] = f(np.sin(freqs).reshape(17, 128, 32)).astype(np.float32)
        hsel = np.zeros((128, 8), dtype=np.float32)
        hsel[:, (core - 1) % 8] = 1.0
        m["halo_sel"] = hsel
        in_maps.append(m)
    return in_maps


_CACHE = {}


def kernel(**inputs):
    in_maps = _host_prep(inputs)
    if "nc" not in _CACHE:
        _CACHE["nc"] = build_program()
    nc = _CACHE["nc"]
    res = run_bass_kernel_spmd(nc, in_maps, core_ids=list(range(8)))
    shards = [res.results[c]["y"] for c in range(8)]
    out = np.stack([np.concatenate(shards[0:4], axis=0),
                    np.concatenate(shards[4:8], axis=0)], axis=0)
    return out.astype(np.float32)


# version 06 (final)
# speedup vs baseline: 1.1387x; 1.1387x over previous
import contextlib
import math
import numpy as np
import concourse.bass as bass
import concourse.mybir as mybir
from concourse.bass_utils import run_bass_kernel_spmd

F32 = mybir.dt.float32
BF16 = mybir.dt.bfloat16
AF = mybir.ActivationFunctionType
ALU = mybir.AluOpType
AX = mybir.AxisListType

D = 1024
T_CORE = 2048
NTILE = 16
C_SQ = float(math.sqrt(1024.0) + 1e-6) ** 2
EPS1 = 1e-5
EPS2 = 1e-5
NEG = -1e30

ENGS = ("pe", "act", "dve", "pool", "sp")


def _small_key(k):
    return k.startswith(("STT", "STB", "ST2", "SMB", "ZT")) or ("STF" in k) or \
        ("_PG" in k) or ("_IND" in k)


class _Op:
    __slots__ = ("eng", "sname", "emit", "deps", "inc", "kind", "cum", "seq", "vol")

    def __init__(self, eng, sname, emit, deps, kind):
        self.vol = False
        self.eng = eng
        self.sname = sname
        self.emit = emit
        self.deps = deps
        self.kind = kind
        self.inc = kind in ("d", "cc")
        self.cum = 0
        self.seq = 0


class Sched:

    def __init__(self, strict=False):
        self.strict = strict
        self.ops = []
        self.by_sem = {}
        self.last_w = {}
        self.readers = {}
        self.barrier = []
        self.stream_last = {}

    def add(self, eng, emit, r=(), w=(), kind="c", extra=(), stream=None):
        sname = eng if kind == "c" else ("cc" if kind == "cc" else stream)
        assert sname is not None
        deps = list(self.barrier) + list(extra)
        if kind in ("d", "cc"):
            prev = self.stream_last.get(sname)
            if prev is not None:
                deps.append(prev)
        for k in r:
            op = self.last_w.get(k)
            if op is not None:
                deps.append(op)
        for k in w:
            op = self.last_w.get(k)
            if op is not None:
                deps.append(op)
            deps.extend(self.readers.get(k, ()))
        o = _Op(eng, sname, emit, deps, kind)
        o.vol = any(_small_key(k) for k in w)
        if kind in ("d", "cc"):
            self.stream_last[sname] = o
        for k in r:
            self.readers.setdefault(k, []).append(o)
        for k in w:
            self.last_w[k] = o
            self.readers[k] = []
        self.ops.append(o)
        lst = self.by_sem.setdefault(sname, [])
        o.cum = (lst[-1].cum if lst else 0) + (16 if kind == "d" else 1)
        lst.append(o)
        return o

    def phase_barrier(self):
        last = {}
        for o in self.ops:
            last[o.sname] = o
        self.barrier = list(last.values())

    def emit_all(self, nc):
        for o in self.ops:
            for d in o.deps:
                if d.kind == "c":
                    if self.strict or d.vol or d.eng != o.eng or o.kind != "c":
                        d.inc = True
                else:
                    d.inc = True
        for sname, lst in self.by_sem.items():
            cum = 0
            for o in lst:
                if o.inc:
                    cum += 16 if o.kind == "d" else 1
                o.cum = cum
        sems = {sname: nc.alloc_semaphore(f"s_{sname}") for sname in self.by_sem}
        per_eng = {e: [o for o in self.ops if o.eng == e] for e in ENGS}

        def runner(ename):
            def body(eng):
                hw = {}
                for o in per_eng[ename]:
                    need = {}
                    for d in o.deps:
                        if d.cum == 0:
                            continue
                        if (not self.strict) and (not d.vol) and d.kind == "c" \
                                and d.eng == ename and o.kind == "c":
                            continue
                        if d.cum > need.get(d.sname, 0):
                            need[d.sname] = d.cum
                    for s2 in sorted(need):
                        v = need[s2]
                        if v > hw.get(s2, 0):
                            eng.wait_ge(sems[s2], v)
                            hw[s2] = v
                    inst = o.emit(eng)
                    if o.inc:
                        assert inst is not None, f"inc-op on {ename} returned no instruction"
                        inst.then_inc(sems[o.sname], 16 if o.kind == "d" else 1)
            return body

        with nc.Block() as block:
            block.tensor(runner("pe"))
            block.scalar(runner("act"))
            block.vector(runner("dve"))
            block.gpsimd(runner("pool"))
            block.sync(runner("sp"))


def build_program(stop_phase=99, strict_sync=False):
    import os as _os
    nc = bass.Bass("TRN2", target_bir_lowering=False)
    S = Sched(strict=strict_sync)

    dI = lambda n, s: nc.dram_tensor(n, list(s), F32, kind="ExternalInput").ap()
    x_in = dI("x", (T_CORE, D))
    qkvT_d = dI("qkvT", (D, 3072))
    projT_d = dI("projT", (D, D))
    conv1T_d = dI("conv1T", (D, 2048))
    conv2T_d = dI("conv2T", (2048, D))
    scale1v = dI("scale1v", (128, 8))
    scale2v = dI("scale2v", (128, 8))
    gammaA_d = dI("gammaA", (1, D))
    projb_d = dI("projb", (1, D))
    conv1b_d = dI("conv1b", (128, 16))
    conv2b_d = dI("conv2b", (1, D))
    WqT_d = dI("WqT", (D, D))
    WkT_d = dI("WkT", (D, 512))
    WvT_d = dI("WvT", (D, 256))
    w1b_s = dI("w1b_s", (8, 8, 128, 512))
    w2b_s = dI("w2b_s", (32, 128, 1024))
    gammaF_d = dI("gammaF", (128, 8))
    betaF_d = dI("betaF", (128, 8))
    WgT_d = dI("WgT", (D, 8))
    bg_d = dI("bg", (1, 8))
    b1v_d = dI("b1v", (128, 8, 32))
    b2r_d = dI("b2r", (1, 8192))
    w1_s = dI("w1_s", (8, 8, 8, 128, 512))
    w2_s = dI("w2_s", (8, 32, 128, 1024))
    identf_d = dI("identf", (128, 128))
    masks_d = dI("masks", (128, 2, 256))
    ropeC_d = dI("ropeC", (17, 128, 32))
    ropeS_d = dI("ropeS", (17, 128, 32))
    cpool_d = dI("cpool", (128, 8))
    ones_d = dI("ones128", (1, 128))
    hsel_d = dI("halo_sel", (128, 8))

    y_out = nc.dram_tensor("y", [T_CORE, D], F32, kind="ExternalOutput").ap()

    kv_ar_in = nc.dram_tensor("kv_ar_in", [128, 520], F32, kind="Internal")
    kv_ar_out = nc.dram_tensor("kv_ar_out", [128, 520], F32, kind="Internal")
    ag_in = nc.dram_tensor("ag_in", [128, 1024], F32, kind="Internal")
    ag_out = nc.dram_tensor("ag_out", [1024, 1024], F32, kind="Internal")
    w1_st = nc.dram_tensor("w1_stash", [8, 8, 8, 128, 512], BF16, kind="Internal")
    w2_st = nc.dram_tensor("w2_stash", [8, 32, 128, 1024], BF16, kind="Internal")
    w1b_st = nc.dram_tensor("w1b_stash", [8, 8, 128, 512], BF16, kind="Internal")
    w2b_st = nc.dram_tensor("w2b_stash", [32, 128, 1024], BF16, kind="Internal")

    al = nc.alloc_sbuf_tensor
    X = al("X", [128, NTILE, D], F32)
    CP = al("CP", [128, 8], F32)
    IDENT = al("IDENT", [128, 128], F32)
    S1V = al("S1V", [128, 8], F32)
    S2V = al("S2V", [128, 8], F32)
    GF = al("GF", [128, 8], F32)
    BFv = al("BFv", [128, 8], F32)
    B1 = al("B1", [128, 8, 32], F32)
    ONE1 = al("ONE1", [1, 128], BF16)
    ONE1F = al("ONE1F", [1, 128], F32)
    WG = al("WG", [128, 8, 8], F32)
    BGR = al("BGR", [1, 8], F32)
    WTM = al("WTM", [128, NTILE, 8], F32)
    HSEL = al("HSEL", [128, 8], F32)

    PQ = [nc.alloc_psum_tensor(f"PQ{j}", [128, 2048], F32) for j in range(2)]
    PP = [PQ[0][:, 0:1024], PQ[0][:, 1024:2048], PQ[1][:, 0:1024], PQ[1][:, 1024:2048]]

    def bank(i):
        return PQ[i // 4][:, (i % 4) * 512:(i % 4) * 512 + 512]

    BK = [f"bk{i}" for i in range(8)]
    PKEY = [[f"bk{2 * j}", f"bk{2 * j + 1}"] for j in range(4)]
    ALLBK = [b for j in range(4) for b in PKEY[j]]

    def dma(eng_name, out_ap, in_ap, r=(), w=(), extra=(), stream="c0"):
        def em(eng, out_ap=out_ap, in_ap=in_ap):
            return eng.dma_start(out=out_ap, in_=in_ap)
        return S.add(eng_name, em, r=r, w=w, kind="d", extra=extra, stream=stream)

    def emit_output():
        yr = y_out.rearrange("(t p) c -> p t c", p=128)
        outops = []
        for i in range(4):
            o = dma("sp", yr[:, 4 * i:4 * i + 4, :], X[:, 4 * i:4 * i + 4, :],
                    r=[f"X{t}" for t in range(4 * i, 4 * i + 4)], w=[f"d_y{i}"],
                    stream="cy")
            outops.append(o)
        S.add("sp", lambda eng: eng.nop(), extra=outops)

    ctxR1 = contextlib.ExitStack()
    sbR1 = lambda n, s, dt: ctxR1.enter_context(nc.sbuf_tensor(n, list(s), dt, side="right"))
    GBC = sbR1("GBC", [128, D], BF16)
    XA = sbR1("XA", [128, NTILE, D], BF16)
    C2BG = sbR1("C2BG", [1, D], BF16)

    ctxR2 = contextlib.ExitStack()
    sbR2 = lambda n, s, dt: ctxR2.enter_context(nc.sbuf_tensor(n, list(s), dt, side="right"))
    QPHI = sbR2("QPHI", [128, 8, T_CORE], BF16)
    KVACC = sbR2("KVACC", [128, 8, 65], F32)
    PBG = sbR2("PBG", [1, D], BF16)
    _stdump = None
    if _os.environ.get("DEVTAP") == "A":
        STD = sbR2("STD", [128, NTILE, 8], F32)
        dbg_st = nc.dram_tensor("dbg_st", [128, NTILE, 8], F32, kind="ExternalOutput").ap()
        _stdump = (STD, dbg_st)

    ctxI = contextlib.ExitStack()
    sbI = lambda n, s, dt: ctxI.enter_context(nc.sbuf_tensor(n, list(s), dt))
    GAR = sbI("GAR", [1, D], F32)
    PBR = sbI("PBR", [1, D], F32)
    C2BR = sbI("C2BR", [1, D], F32)

    xr = x_in.rearrange("(t p) c -> p t c", p=128)
    for i in range(4):
        dma("sp", X[:, 4 * i:4 * i + 4, :], xr[:, 4 * i:4 * i + 4, :],
            w=[f"X{t}" for t in range(4 * i, 4 * i + 4)], stream="cx")
    dma("sp", CP[:, :], cpool_d, w=["CP"])
    dma("sp", IDENT[:, :], identf_d, w=["IDENT"])
    dma("sp", S1V[:, :], scale1v, w=["S1V"])
    dma("sp", S2V[:, :], scale2v, w=["S2V"])
    dma("sp", GF[:, :], gammaF_d, w=["GF"])
    dma("sp", BFv[:, :], betaF_d, w=["BFv"])
    dma("sp", B1[:, :, :], b1v_d, w=["B1"])
    dma("pool", ONE1[:, :], ones_d, w=["ONE1"], stream="p0")
    dma("sp", ONE1F[:, :], ones_d, w=["ONE1F"])
    dma("sp", WG[:, :, :], WgT_d.rearrange("(c p) e -> p c e", p=128), w=["WG"])
    dma("sp", BGR[:, :], bg_d, w=["BGR"])
    dma("sp", GAR[:, :], gammaA_d, w=["GAR"])
    dma("sp", PBR[:, :], projb_d, w=["PBR"])
    dma("sp", C2BR[:, :], conv2b_d, w=["C2BR"])
    dma("sp", HSEL[:, :], hsel_d, w=["HSEL"])

    if _os.environ.get("DEVTAP") == "X":
        dbg_x = nc.dram_tensor("dbg_x", [128, NTILE, D], F32, kind="ExternalOutput").ap()
        dma("sp", dbg_x, X[:, :, :], r=[f"X{t}" for t in range(NTILE)], stream="cy")

    def em_gbc(eng):
        eng.matmul(PP[0][:, 0:512], ONE1F[:, :], GAR[:, 0:512], start=True, stop=True)
        return eng.matmul(PP[0][:, 512:1024], ONE1F[:, :], GAR[:, 512:1024], start=True, stop=True)
    S.add("pe", em_gbc, r=["ONE1F", "GAR"], w=[BK[0], BK[1]])
    S.add("dve", lambda eng: eng.tensor_copy(GBC[:, :], PP[0][:, :]), r=[BK[0], BK[1]], w=["GBC"])
    S.add("dve", lambda eng: eng.tensor_mul(PBG[:, :], GAR[:, :], PBR[:, :]), r=["GAR", "PBR"], w=["PBG"])
    S.add("dve", lambda eng: eng.tensor_mul(C2BG[:, :], GAR[:, :], C2BR[:, :]), r=["GAR", "C2BR"], w=["C2BG"])
    ctxI.close()
    S.phase_barrier()

    def transpose_blocks(src_fn, src_keys, dst_fn, dst_keys, nblk, banks2, affine=None):
        for g in range(0, nblk, 4):
            hi = min(g + 4, nblk)
            bki = banks2[(g // 4) % len(banks2)]

            def em_tr(eng, g=g, hi=hi, bki=bki):
                last = None
                for c in range(g, hi):
                    last = eng.transpose(bank(bki)[:, (c - g) * 128:(c - g) * 128 + 128],
                                         src_fn(c), IDENT[:, :])
                return last
            S.add("pe", em_tr, r=list(src_keys) + ["IDENT"], w=[BK[bki]])

            if affine is None:
                def em_ev(eng, g=g, hi=hi, bki=bki):
                    last = None
                    for c in range(g, hi):
                        last = eng.tensor_copy(dst_fn(c), bank(bki)[:, (c - g) * 128:(c - g) * 128 + 128])
                    return last
                S.add("dve", em_ev, r=[BK[bki]], w=list(dst_keys))
            else:
                gfn, bfn, akeys, extra_fn = affine

                def em_ev(eng, g=g, hi=hi, bki=bki):
                    last = None
                    for c in range(g, hi):
                        src = bank(bki)[:, (c - g) * 128:(c - g) * 128 + 128]
                        last = eng.tensor_scalar(dst_fn(c), src, gfn(c), bfn(c),
                                                 ALU.mult, ALU.add)
                        if extra_fn is not None:
                            last = eng.tensor_scalar(extra_fn(c), src, gfn(c), bfn(c),
                                                     ALU.mult, ALU.add)
                    return last
                S.add("dve", em_ev, r=[BK[bki]] + list(akeys), w=list(dst_keys))

    ctxA = contextlib.ExitStack()
    sbA = lambda n, s, dt: ctxA.enter_context(nc.sbuf_tensor(n, list(s), dt))
    QKVT = sbA("QKVT", [128, 8, 3072], BF16)
    NSC = sbA("NSC", [128, D], F32)
    QSC = sbA("QSC", [128, D], F32)
    AFM = [sbA(f"AFM{i}", [128, 8, 128], BF16) for i in range(2)]
    KPH = sbA("KPH", [128, D], BF16)
    VX = sbA("VX", [128, 16, 65], BF16)
    STT = [sbA(f"STT{i}", [128, 8], F32) for i in range(2)]

    for c in range(8):
        dma("pool", QKVT[:, c, :], qkvT_d.rearrange("(c p) f -> c p f", p=128)[c],
            w=[f"QKVT{c}"], stream="p0")

        def em_fold(eng, c=c):
            return eng.tensor_scalar_mul(QKVT[:, c, :], QKVT[:, c, :], S1V[:, c:c + 1])
        S.add("dve", em_fold, r=[f"QKVT{c}", "S1V"], w=[f"QKVT{c}"])

    S.add("dve", lambda eng: eng.memset(VX[:, :, 64:65], 1.0), w=["VX"])
    S.add("dve", lambda eng: eng.memset(KVACC[:, :, :], 0.0), w=["KVACC"])

    for t in range(NTILE):
        i = t % 2
        st = STT[i]
        sk = f"STT{i}"

        def em_sq(eng, t=t, i=i):
            return eng.activation(NSC[:, :], X[:, t, :], AF.Square, bias=0.0,
                                  scale=1.0, accum_out=STT[i][:, 0:1])
        S.add("act", em_sq, r=[f"X{t}"], w=["NSC", sk])
        S.add("dve", lambda eng, t=t, i=i: eng.reduce_sum(STT[i][:, 1:2], X[:, t, :], axis=AX.X),
              r=[f"X{t}"], w=[sk])

        S.add("dve", lambda eng, st=st: eng.tensor_scalar_mul(st[:, 2:3], st[:, 1:2], 1.0 / 1024.0), r=[sk], w=[sk])
        S.add("dve", lambda eng, st=st: eng.tensor_mul(st[:, 3:4], st[:, 2:3], st[:, 1:2]), r=[sk], w=[sk])
        S.add("dve", lambda eng, st=st: eng.tensor_sub(st[:, 4:5], st[:, 0:1], st[:, 3:4]), r=[sk], w=[sk])
        S.add("dve", lambda eng, st=st: eng.reciprocal(st[:, 5:6], st[:, 4:5]), r=[sk], w=[sk])

        def em_f1(eng, st=st):
            return eng.activation(st[:, 3:4], st[:, 5:6], AF.Sqrt, bias=0.0, scale=CP[:, 1:2])
        S.add("act", em_f1, r=[sk, "CP"], w=[sk])

        S.add("dve", lambda eng, st=st: eng.tensor_scalar(st[:, 5:6], st[:, 4:5], 1.0 / 1024.0, EPS1, ALU.mult, ALU.add), r=[sk], w=[sk])
        S.add("dve", lambda eng, st=st: eng.reciprocal(st[:, 6:7], st[:, 5:6]), r=[sk], w=[sk])

        def em_rstd(eng, st=st):
            return eng.activation(st[:, 5:6], st[:, 6:7], AF.Sqrt, bias=0.0, scale=1.0)
        S.add("act", em_rstd, r=[sk], w=[sk])

        def em_xa(eng, t=t, st=st):
            return eng.tensor_scalar(XA[:, t, :], X[:, t, :], st[:, 2:3], st[:, 5:6],
                                     ALU.subtract, ALU.mult)
        S.add("dve", em_xa, r=[f"X{t}", sk], w=[f"XA{t}"])

        if _stdump is not None:
            def em_std(eng, t=t, st=st):
                return eng.tensor_copy(_stdump[0][:, t, :], st[:, :])
            S.add("dve", em_std, r=[sk], w=["STD"])

        def em_a1(eng, t=t, st=st):
            return eng.tensor_scalar(NSC[:, :], X[:, t, :], st[:, 2:3], st[:, 3:4],
                                     ALU.subtract, ALU.mult)
        S.add("dve", em_a1, r=[f"X{t}", sk, "NSC"], w=["NSC"])

        transpose_blocks(lambda c: NSC[:, c * 128:c * 128 + 128], ["NSC"],
                         lambda c, i=i: AFM[i][:, c, :], [f"AFM{i}"], 8, [6, 7])

        def em_qkv(eng, i=i):
            last = None
            for c in range(8):
                lhs = AFM[i][:, c, :]
                for ns in range(2):
                    last = eng.matmul(PP[1][:, ns * 512:ns * 512 + 512], lhs,
                                      QKVT[:, c, 1024 + ns * 512:1024 + ns * 512 + 512],
                                      start=(c == 0), stop=(c == 7))
                for ns in range(2):
                    last = eng.matmul(PP[2][:, ns * 512:ns * 512 + 512], lhs,
                                      QKVT[:, c, 2048 + ns * 512:2048 + ns * 512 + 512],
                                      start=(c == 0), stop=(c == 7))
                for ns in range(2):
                    last = eng.matmul(PP[0][:, ns * 512:ns * 512 + 512], lhs,
                                      QKVT[:, c, ns * 512:ns * 512 + 512],
                                      start=(c == 0), stop=(c == 7))
            return last
        S.add("pe", em_qkv, r=[f"AFM{i}"] + [f"QKVT{c}" for c in range(8)],
              w=[BK[0], BK[1], BK[2], BK[3], BK[4], BK[5]])

        def em_ksq(eng):
            return eng.activation(NSC[:, :], PP[1][:, :], AF.Square, bias=0.0, scale=1.0)
        S.add("act", em_ksq, r=[BK[2], BK[3], "NSC"], w=["NSC"])

        def em_kexp(eng):
            return eng.activation(KPH[:, :], NSC[:, :], AF.Exp, bias=0.0, scale=CP[:, 0:1])
        S.add("act", em_kexp, r=["NSC", "CP", "KPH"], w=["KPH"])

        def em_vev(eng):
            return eng.tensor_copy(VX[:, :, 0:64],
                                   PP[2][:, :].rearrange("p (h e) -> p h e", h=16))
        S.add("dve", em_vev, r=[BK[4], BK[5], "VX"], w=["VX"])

        def em_qsq(eng):
            return eng.activation(QSC[:, :], PP[0][:, :], AF.Square, bias=0.0, scale=1.0)
        S.add("act", em_qsq, r=[BK[0], BK[1], "QSC"], w=["QSC"])

        def em_qexp(eng):
            return eng.activation(QSC[:, :], QSC[:, :], AF.Exp, bias=0.0, scale=CP[:, 0:1])
        S.add("act", em_qexp, r=["QSC", "CP"], w=["QSC"])

        transpose_blocks(lambda c: QSC[:, c * 128:c * 128 + 128], ["QSC"],
                         lambda c, t=t: QPHI[:, c, t * 128:t * 128 + 128], ["QPHI"], 8, [0, 1])

        for c in range(8):
            bki = 2 + (c % 2)

            def em_kv(eng, c=c, bki=bki):
                last = None
                for hh in range(2):
                    h = 2 * c + hh
                    last = eng.matmul(bank(bki)[hh * 64:hh * 64 + 64, 0:65],
                                      KPH[:, h * 64:h * 64 + 64], VX[:, h, :],
                                      start=True, stop=True)
                return last
            S.add("pe", em_kv, r=["KPH", "VX"], w=[BK[bki]])

            def em_kvadd(eng, c=c, bki=bki):
                return eng.tensor_add(KVACC[:, c, :], KVACC[:, c, :], bank(bki)[:, 0:65])
            S.add("dve", em_kvadd, r=[BK[bki], "KVACC"], w=["KVACC"])

    if stop_phase <= 0:
        ctxA.close()
        ctxR2.close()
        ctxR1.close()
        S.phase_barrier()
        emit_output()
        S.emit_all(nc)
        return nc

    dma("sp", kv_ar_in.ap(), KVACC[:, :, :].rearrange("p c e -> p (c e)"),
        r=["KVACC"], w=["d_kvin"], stream="cl")

    def em_ar(eng):
        return eng.collective_compute(
            "AllReduce", ALU.add,
            replica_groups=[[0, 1, 2, 3], [4, 5, 6, 7]],
            ins=[kv_ar_in.ap().opt()],
            outs=[kv_ar_out.ap().opt()],
        )
    S.add("pool", em_ar, r=["d_kvin"], w=["d_kvout"], kind="cc")

    ctxA.close()
    S.phase_barrier()
    ctxA2 = contextlib.ExitStack()
    sbA2 = lambda n, s, dt: ctxA2.enter_context(nc.sbuf_tensor(n, list(s), dt))
    KVB = sbA2("KVB", [128, 8, 65], F32)
    KVBF = sbA2("KVBF", [128, 8, 65], BF16)
    KSM = sbA2("KSM", [128, 8, 2], BF16)
    PROJT = sbA2("PROJT", [128, 8, 1024], BF16)
    ZT = [sbA2(f"ZT{i}", [128, 16], F32) for i in range(3)]
    OSC = [sbA2(f"OSC{i}", [128, D], F32) for i in range(3)]
    OFM = [sbA2(f"OFM{i}", [128, 8, 128], BF16) for i in range(3)]

    dma("sp", KVB[:, :, :].rearrange("p c e -> p (c e)"), kv_ar_out.ap(),
        r=["d_kvout"], w=["KVB"], stream="cl")
    S.add("dve", lambda eng: eng.tensor_copy(KVBF[:, :, :], KVB[:, :, :]), r=["KVB"], w=["KVBF"])
    S.add("dve", lambda eng: eng.memset(KSM[:, :, :], 0.0), w=["KSM"])

    def em_ksm(eng):
        last = None
        for c in range(8):
            eng.tensor_copy(KSM[0:64, c, 0:1], KVB[0:64, c, 64:65])
            last = eng.tensor_copy(KSM[64:128, c, 1:2], KVB[64:128, c, 64:65])
        return last
    S.add("dve", em_ksm, r=["KVB", "KSM"], w=["KSM"])

    for c in range(8):
        dma("pool", PROJT[:, c, :], projT_d.rearrange("(c p) f -> c p f", p=128)[c],
            w=[f"PROJT{c}"], stream="p0")

        def em_pfold(eng, c=c):
            return eng.tensor_mul(PROJT[:, c, :], PROJT[:, c, :], GBC[:, :])
        S.add("dve", em_pfold, r=[f"PROJT{c}", "GBC"], w=[f"PROJT{c}"])

    if _os.environ.get("DEVTAP") == "A":
        dbg_xa = nc.dram_tensor("dbg_xa", [128, NTILE, D], BF16, kind="ExternalOutput").ap()
        dbg_qp = nc.dram_tensor("dbg_qp", [128, 8, T_CORE], BF16, kind="ExternalOutput").ap()
        dbg_kv = nc.dram_tensor("dbg_kv", [128, 8, 65], F32, kind="ExternalOutput").ap()
        dbg_kvb = nc.dram_tensor("dbg_kvb", [128, 8, 65], F32, kind="ExternalOutput").ap()
        dma("sp", dbg_xa, XA[:, :, :], r=[f"XA{t}" for t in range(NTILE)], stream="cy")
        dma("sp", dbg_qp, QPHI[:, :, :], r=["QPHI"], stream="cy")
        dma("sp", dbg_kv, KVACC[:, :, :], r=["KVACC"], stream="cy")
        dma("sp", dbg_kvb, KVB[:, :, :], r=["KVB"], stream="cy")
        if _stdump is not None:
            dma("sp", _stdump[1], _stdump[0][:, :, :], r=["STD"], stream="cy")

    if stop_phase <= 0.71:
        ctxA2.close()
        ctxR2.close()
        ctxR1.close()
        S.phase_barrier()
        emit_output()
        S.emit_all(nc)
        return nc

    _sub = int(_os.environ.get("PASSB_SUB", "9"))
    passb_tiles = NTILE if stop_phase > 0.9 else max(1, int(round((stop_phase - 0.7) * 40)))

    def passb_stage1(t):
        i = t % 3

        def em_den(eng, t=t):
            last = None
            for c in range(8):
                last = eng.matmul(bank(3)[:, 2 * c:2 * c + 2],
                                  QPHI[:, c, t * 128:t * 128 + 128], KSM[:, c, :],
                                  start=True, stop=True)
            return last
        S.add("pe", em_den, r=["QPHI", "KSM"], w=[BK[3]])

        def em_zadd(eng, i=i):
            return eng.activation(ZT[i][:, :], bank(3)[:, 0:16], AF.Identity,
                                  bias=CP[:, 2:3], scale=1.0)
        S.add("act", em_zadd, r=[BK[3], "CP"], w=[f"ZT{i}"])
        S.add("dve", lambda eng, i=i: eng.reciprocal(ZT[i][:, :], ZT[i][:, :]),
              r=[f"ZT{i}"], w=[f"ZT{i}"])
        if _sub <= 1:
            return

        for c in range(8):
            off = 1024 * (c % 2)
            bks = [BK[4 + 2 * (c % 2)], BK[5 + 2 * (c % 2)]]

            def em_out(eng, t=t, c=c, off=off):
                last = None
                for hh in range(2):
                    last = eng.matmul(PQ[1][:, off + hh * 512:off + hh * 512 + 64],
                                      QPHI[64 * hh:64 * hh + 64, c, t * 128:t * 128 + 128],
                                      KVBF[64 * hh:64 * hh + 64, c, 0:64],
                                      start=True, stop=True)
                return last
            S.add("pe", em_out, r=["QPHI", "KVBF"], w=bks)

            def em_osc(eng, i=i, c=c, off=off):
                pv = PQ[1][:, off:off + 1024].rearrange("p (h x) -> p h x", h=2)[:, :, 0:64]
                return eng.tensor_tensor(
                    OSC[i][:, c * 128:c * 128 + 128].rearrange("p (h e) -> p h e", h=2),
                    pv,
                    ZT[i][:, 2 * c:2 * c + 2].unsqueeze(2).broadcast_to([128, 2, 64]),
                    ALU.mult)
            S.add("dve", em_osc, r=bks + [f"ZT{i}", f"OSC{i}"], w=[f"OSC{i}"])

        if _sub <= 2:
            return
        transpose_blocks(lambda c, i=i: OSC[i][:, c * 128:c * 128 + 128], [f"OSC{i}"],
                         lambda c, i=i: OFM[i][:, c, :], [f"OFM{i}"], 8, [4, 5])

    def passb_stage2(t):
        i = t % 3

        def em_proj(eng, i=i):
            last = None
            for c in range(8):
                for ns in range(2):
                    last = eng.matmul(PP[0][:, ns * 512:ns * 512 + 512], OFM[i][:, c, :],
                                      PROJT[:, c, ns * 512:ns * 512 + 512],
                                      start=(c == 0), stop=False)
            for ns in range(2):
                last = eng.matmul(PP[0][:, ns * 512:ns * 512 + 512], ONE1[:, :],
                                  PBG[:, ns * 512:ns * 512 + 512], start=False, stop=True)
            return last
        S.add("pe", em_proj, r=[f"OFM{i}", "ONE1", "PBG"] + [f"PROJT{c}" for c in range(8)],
              w=[BK[0], BK[1]])

        def em_x1a(eng, t=t, i=i):
            eng.tensor_copy(OSC[i][:, :], XA[:, t, :])
            return eng.tensor_add(OSC[i][:, :], OSC[i][:, :], PP[0][:, :])
        S.add("dve", em_x1a, r=[BK[0], BK[1], f"XA{t}", f"OSC{i}"], w=[f"OSC{i}"])

        def em_x1b(eng, t=t, i=i):
            return eng.tensor_add(X[:, t, :], X[:, t, :], OSC[i][:, :])
        S.add("dve", em_x1b, r=[f"OSC{i}", f"X{t}"], w=[f"X{t}"])

        def em_x1c(eng, t=t, i=i):
            return eng.copy(XA[:, t, :], OSC[i][:, :])
        S.add("act", em_x1c, r=[f"OSC{i}"], w=[f"XA{t}"])

    passb_stage1(0)
    if passb_tiles > 1:
        passb_stage1(1)
    for t in range(passb_tiles):
        if t + 2 < passb_tiles:
            passb_stage1(t + 2)
        if _sub >= 4:
            passb_stage2(t)

    ctxA2.close()
    ctxR2.close()
    S.phase_barrier()
    if stop_phase <= 1:
        ctxR1.close()
        emit_output()
        S.emit_all(nc)
        return nc

    ctxC = contextlib.ExitStack()
    sbC = lambda n, s, dt: ctxC.enter_context(nc.sbuf_tensor(n, list(s), dt))
    CONV1T = sbC("CONV1T", [128, 8, 2048], BF16)
    CONV2T = sbC("CONV2T", [128, 16, 1024], BF16)
    C1B = sbC("C1B", [128, 16], F32)
    NSC2 = sbC("NSC2", [128, D], F32)
    AFM4 = sbC("AFM4", [128, 8, 512], BF16)
    H1 = sbC("H1", [128, 16, 512], BF16)
    ST2 = [sbC(f"ST2_{i}", [128, 4], F32) for i in range(2)]

    dma("sp", C1B[:, :], conv1b_d, w=["C1B"])
    for c in range(8):
        dma("pool", CONV1T[:, c, :], conv1T_d.rearrange("(c p) f -> c p f", p=128)[c],
            w=[f"CONV1T{c}"], stream="p0")

        def em_c1fold(eng, c=c):
            return eng.tensor_scalar_mul(CONV1T[:, c, :], CONV1T[:, c, :], S2V[:, c:c + 1])
        S.add("dve", em_c1fold, r=[f"CONV1T{c}", "S2V"], w=[f"CONV1T{c}"])
    for c in range(16):
        dma("pool", CONV2T[:, c, :], conv2T_d.rearrange("(c p) f -> c p f", p=128)[c],
            w=[f"CONV2T{c}"], stream="p0")

        def em_c2fold(eng, c=c):
            return eng.tensor_mul(CONV2T[:, c, :], CONV2T[:, c, :], GBC[:, :])
        S.add("dve", em_c2fold, r=[f"CONV2T{c}", "GBC"], w=[f"CONV2T{c}"])

    for q in range(4):
        iq = q % 2
        for tl in range(4):
            t = q * 4 + tl

            def em_sq2(eng, t=t, iq=iq, tl=tl):
                return eng.activation(NSC2[:, :], XA[:, t, :], AF.Square, bias=0.0,
                                      scale=1.0, accum_out=ST2[iq][:, tl:tl + 1])
            S.add("act", em_sq2, r=[f"XA{t}", "NSC2"], w=["NSC2", f"ST2_{iq}"])

            def em_rr(eng, iq=iq, tl=tl):
                return eng.reciprocal(ST2[iq][:, tl:tl + 1], ST2[iq][:, tl:tl + 1])
            S.add("dve", em_rr, r=[f"ST2_{iq}"], w=[f"ST2_{iq}"])

            def em_f2(eng, iq=iq, tl=tl):
                return eng.activation(ST2[iq][:, tl:tl + 1], ST2[iq][:, tl:tl + 1], AF.Sqrt,
                                      bias=0.0, scale=CP[:, 1:2])
            S.add("act", em_f2, r=[f"ST2_{iq}", "CP"], w=[f"ST2_{iq}"])

            def em_a2(eng, t=t, iq=iq, tl=tl):
                return eng.tensor_scalar_mul(NSC2[:, :], XA[:, t, :], ST2[iq][:, tl:tl + 1])
            S.add("dve", em_a2, r=[f"XA{t}", f"ST2_{iq}", "NSC2"], w=["NSC2"])

            transpose_blocks(lambda c: NSC2[:, c * 128:c * 128 + 128], ["NSC2"],
                             lambda c, tl=tl: AFM4[:, c, tl * 128:tl * 128 + 128],
                             ["AFM4"], 8, [0, 1])

        for fc in range(16):
            bki = fc % 4

            def em_c1(eng, fc=fc, bki=bki):
                last = None
                for c in range(8):
                    last = eng.matmul(bank(bki)[:, :], CONV1T[:, c, fc * 128:fc * 128 + 128],
                                      AFM4[:, c, :], start=(c == 0), stop=(c == 7))
                return last
            S.add("pe", em_c1, r=["AFM4"] + [f"CONV1T{c}" for c in range(8)], w=[BK[bki]])

            def em_gelu(eng, fc=fc, bki=bki):
                return eng.activation(H1[:, fc, :], bank(bki)[:, :], AF.Gelu,
                                      bias=C1B[:, fc:fc + 1], scale=1.0)
            S.add("act", em_gelu, r=[BK[bki], "C1B", "H1"], w=["H1"])

        for tl in range(4):
            t = q * 4 + tl
            pj = 2 + (tl % 2)

            def em_c2(eng, tl=tl, pj=pj):
                last = None
                for c in range(16):
                    for ns in range(2):
                        last = eng.matmul(PP[pj][:, ns * 512:ns * 512 + 512],
                                          H1[:, c, tl * 128:tl * 128 + 128],
                                          CONV2T[:, c, ns * 512:ns * 512 + 512],
                                          start=(c == 0), stop=False)
                for ns in range(2):
                    last = eng.matmul(PP[pj][:, ns * 512:ns * 512 + 512], ONE1[:, :],
                                      C2BG[:, ns * 512:ns * 512 + 512], start=False, stop=True)
                return last
            S.add("pe", em_c2, r=["H1", "ONE1", "C2BG"] + [f"CONV2T{c}" for c in range(16)],
                  w=PKEY[pj])

            def em_xup(eng, t=t, pj=pj):
                return eng.tensor_add(X[:, t, :], X[:, t, :], PP[pj][:, :])
            S.add("dve", em_xup, r=PKEY[pj] + [f"X{t}"], w=[f"X{t}"])

    ctxC.close()
    ctxR1.close()
    S.phase_barrier()
    if stop_phase <= 2:
        emit_output()
        S.emit_all(nc)
        return nc

    def ffn_call(callno):
        ctx = contextlib.ExitStack()
        pfx = f"f{callno}_"
        sbF = lambda n, s, dt: ctx.enter_context(nc.sbuf_tensor(pfx + n, list(s), dt))
        XN = sbF("XN", [128, 8, T_CORE], BF16)
        XNF = [sbF(pfx + "XNF0", [128, 8, 128], F32)] * 2
        NSCF = [sbF(f"NSCF{i}", [128, D], F32) for i in range(2)]
        STF = [sbF(f"STF{i}", [128, 8], F32) for i in range(2)]
        PGb = [sbF(f"PG{i}", [128, 8], F32) for i in range(2)]
        INDb = [sbF(f"IND{i}", [128, 8], F32) for i in range(2)]
        W1S = [sbF(f"W1S{i}", [128, 8, 512], BF16) for i in range(2)]
        W2S = [sbF(pfx + f"W2S{i}", [128, 4, 1024], BF16) for i in range(2)]
        HFb = sbF("HF", [128, 32, 512], BF16)
        B2 = sbF("B2", [1, 8, 1024], BF16)

        dma("pool", B2[:, :, :], b2r_d.rearrange("r (e o) -> r e o", e=8), w=[pfx + "B2"], stream="p0")

        def ln_tile(t):
            i = t % 2
            st = STF[i]
            sk = pfx + f"STF{i}"

            def em_sq(eng, t=t, i=i):
                return eng.activation(NSCF[i][:, :], X[:, t, :], AF.Square, bias=0.0,
                                      scale=1.0, accum_out=STF[i][:, 0:1])
            S.add("act", em_sq, r=[f"X{t}", pfx + f"NSCF{i}"], w=[pfx + f"NSCF{i}", sk])
            S.add("dve", lambda eng, t=t, i=i: eng.reduce_sum(STF[i][:, 1:2], X[:, t, :], axis=AX.X),
                  r=[f"X{t}"], w=[sk])

            S.add("dve", lambda eng, st=st: eng.tensor_scalar_mul(st[:, 2:3], st[:, 1:2], 1.0 / 1024.0), r=[sk], w=[sk])
            S.add("dve", lambda eng, st=st: eng.tensor_mul(st[:, 3:4], st[:, 2:3], st[:, 1:2]), r=[sk], w=[sk])
            S.add("dve", lambda eng, st=st: eng.tensor_sub(st[:, 3:4], st[:, 0:1], st[:, 3:4]), r=[sk], w=[sk])
            S.add("dve", lambda eng, st=st: eng.tensor_scalar(st[:, 5:6], st[:, 3:4], (1.0 + EPS2) / 1024.0, EPS1 * EPS2, ALU.mult, ALU.add), r=[sk], w=[sk])
            S.add("dve", lambda eng, st=st: eng.reciprocal(st[:, 4:5], st[:, 5:6]), r=[sk], w=[sk])
            S.add("dve", lambda eng, st=st: eng.tensor_scalar(st[:, 5:6], st[:, 3:4], 1.0 / 1024.0, EPS1, ALU.mult, ALU.add), r=[sk], w=[sk])
            S.add("dve", lambda eng, st=st: eng.reciprocal(st[:, 7:8], st[:, 5:6]), r=[sk], w=[sk])

            def em_gs(eng, st=st):
                eng.activation(st[:, 4:5], st[:, 4:5], AF.Sqrt, bias=0.0, scale=1.0)
                return eng.activation(st[:, 7:8], st[:, 7:8], AF.Sqrt, bias=0.0, scale=1.0)
            S.add("act", em_gs, r=[sk], w=[sk])

            def em_ln(eng, t=t, i=i, st=st):
                return eng.tensor_scalar(NSCF[i][:, :], X[:, t, :], st[:, 2:3], st[:, 4:5],
                                         ALU.subtract, ALU.mult)
            S.add("dve", em_ln, r=[f"X{t}", sk, pfx + f"NSCF{i}"], w=[pfx + f"NSCF{i}"])

            transpose_blocks(lambda c, i=i: NSCF[i][:, c * 128:c * 128 + 128], [pfx + f"NSCF{i}"],
                             lambda c, t=t: XN[:, c, t * 128:t * 128 + 128],
                             [pfx + "XN", pfx + "XNF0"], 8, [0, 1],
                             affine=(lambda c: GF[:, c:c + 1], lambda c: BFv[:, c:c + 1],
                                     ["GF", "BFv"],
                                     lambda c, i=i: XNF[i][:, c, :]))

            def em_lnres(eng, t=t, i=i, st=st):
                return eng.tensor_scalar(NSCF[i][:, :], X[:, t, :], st[:, 2:3], st[:, 7:8],
                                         ALU.subtract, ALU.mult)
            S.add("dve", em_lnres, r=[f"X{t}", sk, pfx + f"NSCF{i}"], w=[pfx + f"NSCF{i}"])

            def em_lnadd(eng, t=t, i=i):
                return eng.tensor_add(X[:, t, :], X[:, t, :], NSCF[i][:, :])
            S.add("dve", em_lnadd, r=[f"X{t}", pfx + f"NSCF{i}"], w=[f"X{t}"])

            def em_gl(eng, i=i):
                last = None
                for c in range(8):
                    last = eng.matmul(bank(7)[:, 0:8], XNF[i][:, c, :], WG[:, c, :],
                                      start=(c == 0), stop=False)
                return eng.matmul(bank(7)[:, 0:8], ONE1F[:, :], BGR[:, :],
                                  start=False, stop=True)
            S.add("pe", em_gl, r=[pfx + "XNF0", "WG", "ONE1F", "BGR"], w=[BK[7]])

            def em_nm(eng, st=st):
                return eng.reduce_max(st[:, 5:6], bank(7)[:, 0:8], axis=AX.X, negate=True)
            S.add("dve", em_nm, r=[BK[7], sk], w=[sk])

            def em_exp(eng, i=i, st=st):
                return eng.activation(PGb[i][:, :], bank(7)[:, 0:8], AF.Exp,
                                      bias=st[:, 5:6], scale=1.0, accum_out=st[:, 6:7])
            S.add("act", em_exp, r=[BK[7], sk, pfx + f"PG{i}"], w=[pfx + f"PG{i}", sk])

            P = PGb[i][:, :]
            IN = INDb[i][:, :]
            pk = pfx + f"PG{i}"
            ik = pfx + f"IND{i}"
            S.add("dve", lambda eng, P=P, st=st: eng.reduce_max(st[:, 0:1], P, axis=AX.X), r=[pk, sk], w=[sk])
            S.add("dve", lambda eng, P=P, IN=IN, st=st: eng.tensor_single_scalar(IN, P, st[:, 0:1], ALU.is_ge), r=[pk, sk, ik], w=[ik])
            S.add("dve", lambda eng, P=P, IN=IN: eng.scalar_tensor_tensor(IN, IN, NEG, P, ALU.mult, ALU.add), r=[pk, ik], w=[ik])
            S.add("dve", lambda eng, IN=IN, st=st: eng.reduce_max(st[:, 1:2], IN, axis=AX.X), r=[ik, sk], w=[sk])
            S.add("dve", lambda eng, P=P, IN=IN, st=st: eng.tensor_single_scalar(IN, P, st[:, 1:2], ALU.is_ge), r=[pk, sk, ik], w=[ik])
            S.add("dve", lambda eng, st=st: eng.tensor_add(st[:, 2:3], st[:, 0:1], st[:, 1:2]), r=[sk], w=[sk])
            S.add("dve", lambda eng, st=st: eng.scalar_tensor_tensor(st[:, 2:3], st[:, 6:7], 1e-8, st[:, 2:3], ALU.mult, ALU.add), r=[sk], w=[sk])
            S.add("dve", lambda eng, st=st: eng.reciprocal(st[:, 2:3], st[:, 2:3]), r=[sk], w=[sk])
            S.add("dve", lambda eng, P=P, IN=IN, st=st, t=t: eng.scalar_tensor_tensor(WTM[:, t, :], P, st[:, 2:3], IN, ALU.mult, ALU.mult), r=[pk, sk, ik], w=["WTM"])

        for t in range(4):
            ln_tile(t)
        w1c = [0]
        w2c = [0]
        first = (callno == 1)
        for tb in range(4):
            for e in range(8):
                for fs in range(8):
                    j = w1c[0] % 2
                    w1c[0] += 1
                    wkey = pfx + f"W1S{j}"
                    if first and tb == 0:
                        dma("pool", W1S[j][:, :, :], w1_s[e, fs].rearrange("c p x -> p c x"),
                            w=[wkey], stream=f"w1{j}")
                        dma("sp", w1_st.ap()[e, fs].rearrange("c p x -> p c x"),
                            W1S[j][:, :, :], r=[wkey], w=[f"stw1_{e}_{fs}"], stream="sw1")
                    else:
                        dma("sp", W1S[j][:, :, :],
                            w1_st.ap()[e, fs].rearrange("c p x -> p c x"),
                            r=[f"stw1_{e}_{fs}"], w=[wkey], stream=f"w1{j}")
                    for fcl in range(4):
                        fc = fs * 4 + fcl
                        bki = fc % 4

                        def em_h(eng, j=j, fcl=fcl, tb=tb, bki=bki):
                            last = None
                            for c in range(8):
                                last = eng.matmul(bank(bki)[:, :],
                                                  W1S[j][:, c, fcl * 128:fcl * 128 + 128],
                                                  XN[:, c, tb * 512:tb * 512 + 512],
                                                  start=(c == 0), stop=(c == 7))
                            return last
                        S.add("pe", em_h, r=[wkey, pfx + "XN"], w=[BK[bki]])

                        def em_gel(eng, e=e, fc=fc, bki=bki):
                            return eng.activation(HFb[:, fc, :], bank(bki)[:, :], AF.Gelu,
                                                  bias=B1[:, e, fc:fc + 1], scale=1.0)
                        S.add("act", em_gel, r=[BK[bki], "B1"], w=[pfx + f"HF{fc}"])

                if e == 0 and tb + 1 < 4:
                    for t in range(4 * (tb + 1), 4 * (tb + 1) + 4):
                        ln_tile(t)
                for fc in range(32):
                    fg, fl = divmod(fc, 4)
                    j2 = (w2c[0] // 4) % 2
                    w2c[0] += 1
                    w2key = pfx + f"W2S{j2}"
                    if fl == 0:
                        if first and tb == 0:
                            dma("pool", W2S[j2][:, :, :],
                                w2_s[e, 4 * fg:4 * fg + 4].rearrange("f p o -> p f o"),
                                w=[w2key], stream=f"w2{j2}")
                            dma("sp", w2_st.ap()[e, 4 * fg:4 * fg + 4].rearrange("f p o -> p f o"),
                                W2S[j2][:, :, :], r=[w2key], w=[f"stw2_{e}_{fg}"], stream="sw2")
                        else:
                            dma("sp", W2S[j2][:, :, :],
                                w2_st.ap()[e, 4 * fg:4 * fg + 4].rearrange("f p o -> p f o"),
                                r=[f"stw2_{e}_{fg}"], w=[w2key], stream=f"w2{j2}")

                    def em_y(eng, j2=j2, fc=fc, fl=fl):
                        last = None
                        for tl in range(4):
                            for ns in range(2):
                                last = eng.matmul(PP[tl][:, ns * 512:ns * 512 + 512],
                                                  HFb[:, fc, tl * 128:tl * 128 + 128],
                                                  W2S[j2][:, fl, ns * 512:ns * 512 + 512],
                                                  start=(fc == 0), stop=False)
                        return last
                    S.add("pe", em_y, r=[w2key, pfx + f"HF{fc}"], w=ALLBK)

                def em_yb(eng, e=e):
                    last = None
                    for tl in range(4):
                        for ns in range(2):
                            last = eng.matmul(PP[tl][:, ns * 512:ns * 512 + 512], ONE1[:, :],
                                              B2[:, e, ns * 512:ns * 512 + 512],
                                              start=False, stop=True)
                    return last
                S.add("pe", em_yb, r=["ONE1", pfx + "B2"], w=ALLBK)

                for tl in range(4):
                    t = tb * 4 + tl

                    def em_comb(eng, t=t, tl=tl, e=e):
                        return eng.scalar_tensor_tensor(X[:, t, :], PP[tl][:, :],
                                                        WTM[:, t, e:e + 1], X[:, t, :],
                                                        ALU.mult, ALU.add)
                    S.add("dve", em_comb, r=PKEY[tl] + ["WTM", f"X{t}"], w=[f"X{t}"])

        ctx.close()
        S.phase_barrier()

    ctxPB = contextlib.ExitStack()
    sbPB = lambda n, s, dt: ctxPB.enter_context(nc.sbuf_tensor(n, list(s), dt))
    WKT = sbPB("WKT", [128, 8, 512], BF16)
    WVT = sbPB("WVT", [128, 8, 256], BF16)
    for c in range(8):
        dma("pool", WKT[:, c, :], WkT_d.rearrange("(c p) f -> c p f", p=128)[c], w=["WKT"], stream="p0")
        dma("pool", WVT[:, c, :], WvT_d.rearrange("(c p) f -> c p f", p=128)[c], w=["WVT"], stream="p0")

    ffn_call(1)
    if stop_phase <= 3:
        emit_output()
        S.emit_all(nc)
        return nc

    dma("sp", ag_in.ap(), X[:, 15, :], r=["X15"], w=["d_agin"], stream="cl")

    def em_ag(eng):
        return eng.collective_compute(
            "AllGather", ALU.bypass,
            replica_groups=[list(range(8))],
            ins=[ag_in.ap().opt()],
            outs=[ag_out.ap().opt()],
        )
    S.add("pool", em_ag, r=["d_agin"], w=["d_agout"], kind="cc")

    ctxR4 = contextlib.ExitStack()
    sbR4 = lambda n, s, dt: ctxR4.enter_context(nc.sbuf_tensor(n, list(s), dt, side="right"))
    QFM = sbR4("QFM", [128, 8, 2048], BF16)
    KFM = sbR4("KFM", [128, 4, 2176], BF16)
    VTM = sbR4("VTM", [128, 17, 256], BF16)
    MASKS = sbR4("MASKS", [128, 2, 256], F32)

    ctxB = contextlib.ExitStack()
    sbB = lambda n, s, dt: ctxB.enter_context(nc.sbuf_tensor(n, list(s), dt))
    AFMB = [sbB(f"AFMB{i}", [128, 8, 128], BF16) for i in range(2)]
    WQT = sbB("WQT", [128, 8, 1024], BF16)
    ROPEC = sbB("ROPEC", [128, 17, 32], BF16)
    ROPES = sbB("ROPES", [128, 17, 32], BF16)
    NSCB = sbB("NSCB", [128, D], F32)
    QSCB = sbB("QSCB", [128, D], F32)
    KSCB = sbB("KSCB", [128, 512], F32)
    TMPR = sbB("TMPR", [128, 1024], F32)
    STB = sbB("STB", [128, 8], F32)
    HALO = sbB("HALO", [128, D], F32)
    SLC = QSCB

    dma("sp", MASKS[:, :, :], masks_d, w=["MASKS"])
    dma("pool", ROPEC[:, :, :], ropeC_d.rearrange("t p f -> p t f"), w=["ROPEC"], stream="p0")
    dma("pool", ROPES[:, :, :], ropeS_d.rearrange("t p f -> p t f"), w=["ROPES"], stream="p0")
    for c in range(8):
        dma("pool", WQT[:, c, :], WqT_d.rearrange("(c p) f -> c p f", p=128)[c], w=[f"WQT{c}"], stream="p0")

        def em_qfold(eng, c=c):
            return eng.tensor_scalar_mul(WQT[:, c, :], WQT[:, c, :], 0.125)
        S.add("dve", em_qfold, r=[f"WQT{c}"], w=[f"WQT{c}"])

    for r8 in range(8):
        dma("sp", SLC[:, :], ag_out.ap()[r8 * 128:r8 * 128 + 128, :],
            r=["d_agout", "QSCB"], w=["QSCB"], stream="cl")
        if r8 == 0:
            def em_h0(eng):
                return eng.tensor_scalar_mul(HALO[:, :], SLC[:, :], HSEL[:, 0:1])
            S.add("dve", em_h0, r=["QSCB", "HSEL", "HALO"], w=["HALO"])
        else:
            def em_hr(eng, r8=r8):
                return eng.scalar_tensor_tensor(HALO[:, :], SLC[:, :], HSEL[:, r8:r8 + 1],
                                                HALO[:, :], ALU.mult, ALU.add)
            S.add("dve", em_hr, r=["QSCB", "HSEL", "HALO"], w=["HALO"])

    def b_rope(psum_keys, psum_view, nh, tt, dst, dst_key):
        cosb = ROPEC[:, tt, :].unsqueeze(1).broadcast_to([128, nh, 32])
        sinb = ROPES[:, tt, :].unsqueeze(1).broadcast_to([128, nh, 32])
        x1 = psum_view[:, :, 0:32]
        x2 = psum_view[:, :, 32:64]
        ta = TMPR[:, 0:nh * 32].rearrange("p (h f) -> p h f", h=nh)
        tbv = TMPR[:, nh * 32:nh * 64].rearrange("p (h f) -> p h f", h=nh)
        dv = dst.rearrange("p (h f) -> p h f", h=nh)

        def em(eng):
            eng.tensor_tensor(ta, x1, cosb, ALU.mult)
            eng.tensor_tensor(tbv, x2, sinb, ALU.mult)
            eng.tensor_sub(dv[:, :, 0:32], ta, tbv)
            eng.tensor_tensor(ta, x1, sinb, ALU.mult)
            eng.tensor_tensor(tbv, x2, cosb, ALU.mult)
            return eng.tensor_add(dv[:, :, 32:64], ta, tbv)
        S.add("dve", em, r=list(psum_keys) + ["ROPEC", "ROPES", "TMPR", dst_key],
              w=[dst_key, "TMPR"])

    for tt in list(range(1, 17)) + [0]:
        src = HALO[:, :] if tt == 0 else X[:, tt - 1, :]
        srck = "HALO" if tt == 0 else f"X{tt - 1}"
        ib = tt % 2

        def em_sq(eng, src=src):
            return eng.activation(NSCB[:, :], src, AF.Square, bias=0.0,
                                  scale=1.0, accum_out=STB[:, 0:1])
        S.add("act", em_sq, r=[srck, "NSCB"], w=["NSCB", "STB"])
        S.add("dve", lambda eng, src=src: eng.reduce_sum(STB[:, 1:2], src, axis=AX.X),
              r=[srck, "STB"], w=["STB"])

        S.add("dve", lambda eng: eng.tensor_scalar_mul(STB[:, 2:3], STB[:, 1:2], 1.0 / 1024.0), r=["STB"], w=["STB"])
        S.add("dve", lambda eng: eng.tensor_mul(STB[:, 3:4], STB[:, 2:3], STB[:, 1:2]), r=["STB"], w=["STB"])
        S.add("dve", lambda eng: eng.tensor_sub(STB[:, 3:4], STB[:, 0:1], STB[:, 3:4]), r=["STB"], w=["STB"])
        S.add("dve", lambda eng: eng.tensor_scalar(STB[:, 3:4], STB[:, 3:4], 1.0 / 1024.0, EPS1, ALU.mult, ALU.add), r=["STB"], w=["STB"])
        S.add("dve", lambda eng: eng.reciprocal(STB[:, 4:5], STB[:, 3:4]), r=["STB"], w=["STB"])
        S.add("act", lambda eng: eng.activation(STB[:, 4:5], STB[:, 4:5], AF.Sqrt,
                                                bias=0.0, scale=1.0), r=["STB"], w=["STB"])

        def em_ln(eng, src=src):
            return eng.tensor_scalar(NSCB[:, :], src, STB[:, 2:3], STB[:, 4:5],
                                     ALU.subtract, ALU.mult)
        S.add("dve", em_ln, r=[srck, "STB", "NSCB"], w=["NSCB"])

        transpose_blocks(lambda c: NSCB[:, c * 128:c * 128 + 128], ["NSCB"],
                         lambda c, ib=ib: AFMB[ib][:, c, :], [f"AFMB{ib}"], 8, [0, 1])

        def em_kmm(eng, ib=ib):
            last = None
            for c in range(8):
                last = eng.matmul(bank(4)[:, 0:512], AFMB[ib][:, c, :],
                                  WKT[:, c, :], start=(c == 0), stop=(c == 7))
            return last
        S.add("pe", em_kmm, r=[f"AFMB{ib}", "WKT"], w=[BK[4]])
        b_rope([BK[4]], bank(4)[:, 0:512].rearrange("p (h f) -> p h f", h=8), 8, tt,
               KSCB[:, :], "KSCB")
        transpose_blocks(lambda c: KSCB[:, c * 128:c * 128 + 128], ["KSCB"],
                         lambda c, tt=tt: KFM[:, c, tt * 128:tt * 128 + 128], ["KFM"], 4, [6])

        def em_vmm(eng, ib=ib):
            last = None
            for c in range(8):
                last = eng.matmul(bank(5)[:, 0:256], AFMB[ib][:, c, :],
                                  WVT[:, c, :], start=(c == 0), stop=(c == 7))
            return last
        S.add("pe", em_vmm, r=[f"AFMB{ib}", "WVT"], w=[BK[5]])

        def em_vev(eng, tt=tt):
            return eng.tensor_copy(VTM[:, tt, :], bank(5)[:, 0:256])
        S.add("dve", em_vev, r=[BK[5], "VTM"], w=["VTM"])

        if tt >= 1:
            t = tt - 1

            def em_qmm(eng, ib=ib):
                last = None
                for c in range(8):
                    for ns in range(2):
                        last = eng.matmul(PP[1][:, ns * 512:ns * 512 + 512],
                                          AFMB[ib][:, c, :],
                                          WQT[:, c, ns * 512:ns * 512 + 512],
                                          start=(c == 0), stop=(c == 7))
                return last
            S.add("pe", em_qmm, r=[f"AFMB{ib}"] + [f"WQT{c}" for c in range(8)],
                  w=[BK[2], BK[3]])
            b_rope([BK[2], BK[3]], PP[1][:, :].rearrange("p (h f) -> p h f", h=16), 16, tt,
                   QSCB[:, :], "QSCB")
            transpose_blocks(lambda c: QSCB[:, c * 128:c * 128 + 128], ["QSCB"],
                             lambda c, t=t: QFM[:, c, t * 128:t * 128 + 128], ["QFM"], 8, [6, 7])

    ctxB.close()
    ctxPB.close()
    S.phase_barrier()

    ctxR3 = contextlib.ExitStack()
    OFMB = ctxR3.enter_context(nc.sbuf_tensor("OFMB", [128, 8, 2048], BF16))
    ctxB2 = contextlib.ExitStack()
    sbB2 = lambda n, s, dt: ctxB2.enter_context(nc.sbuf_tensor(n, list(s), dt))
    SSB = [sbB2(f"SSB{i}", [128, 4, 256], F32) for i in range(2)]
    PFB = [sbB2(f"PFB{i}", [128, 4, 256], F32) for i in range(2)]
    PTS = [sbB2(f"PTS{i}", [128, 8, 128], BF16) for i in range(2)]
    SMB = [sbB2(f"SMB{i}", [128, 8], F32) for i in range(2)]

    def sc_stage1(g):
        t, hg = divmod(g, 4)
        msel = 0 if t == 0 else 1
        i = g % 2
        par = g % 2
        PB = PQ[par]
        bkeys = [BK[4 * par + k] for k in range(4)]

        def em_sc(eng, t=t, hg=hg, PB=PB):
            last = None
            for j in range(4):
                h = hg * 4 + j
                hkv = h // 4
                last = eng.matmul(
                    PB[:, j * 512:j * 512 + 256],
                    QFM[64 * (h % 2):64 * (h % 2) + 64, h // 2, t * 128:t * 128 + 128],
                    KFM[64 * (h % 2):64 * (h % 2) + 64, hkv, t * 128:t * 128 + 256],
                    start=True, stop=True)
            return last
        S.add("pe", em_sc, r=["QFM", "KFM"], w=bkeys)

        def em_mask(eng, i=i, msel=msel, PB=PB):
            pv = PB[:, :].rearrange("p (j x) -> p j x", j=4)[:, :, 0:256]
            return eng.tensor_tensor(SSB[i][:, :, :], pv,
                                     MASKS[:, msel, :].unsqueeze(1).broadcast_to([128, 4, 256]),
                                     ALU.add)
        S.add("dve", em_mask, r=bkeys + ["MASKS", f"SSB{i}"], w=[f"SSB{i}"])

        def em_nm(eng, i=i):
            return eng.tensor_reduce(SMB[i][:, 0:4], SSB[i][:, :, :], axis=AX.X,
                                     op=ALU.max, negate=True)
        S.add("dve", em_nm, r=[f"SSB{i}", f"SMB{i}"], w=[f"SMB{i}"])

        def em_sub(eng, i=i):
            return eng.tensor_tensor(SSB[i][:, :, :], SSB[i][:, :, :],
                                     SMB[i][:, 0:4].unsqueeze(2).broadcast_to([128, 4, 256]),
                                     ALU.add)
        S.add("dve", em_sub, r=[f"SSB{i}", f"SMB{i}"], w=[f"SSB{i}"])

        def em_exp(eng, i=i):
            return eng.activation(SSB[i][:, :, :].rearrange("p j k -> p (j k)"),
                                  SSB[i][:, :, :].rearrange("p j k -> p (j k)"),
                                  AF.Exp, bias=0.0, scale=1.0)
        S.add("act", em_exp, r=[f"SSB{i}"], w=[f"SSB{i}"])

        def em_sm(eng, i=i):
            return eng.tensor_reduce(SMB[i][:, 4:8], SSB[i][:, :, :], axis=AX.X, op=ALU.add)
        S.add("dve", em_sm, r=[f"SSB{i}", f"SMB{i}"], w=[f"SMB{i}"])
        S.add("dve", lambda eng, i=i: eng.reciprocal(SMB[i][:, 4:8], SMB[i][:, 4:8]),
              r=[f"SMB{i}"], w=[f"SMB{i}"])

        def em_pn(eng, i=i):
            return eng.tensor_tensor(PFB[i][:, :, :], SSB[i][:, :, :],
                                     SMB[i][:, 4:8].unsqueeze(2).broadcast_to([128, 4, 256]),
                                     ALU.mult)
        S.add("dve", em_pn, r=[f"SSB{i}", f"SMB{i}", f"PFB{i}"], w=[f"PFB{i}"])

    def sc_stage2(g):
        t, hg = divmod(g, 4)
        i = g % 2
        par = g % 2
        PB = PQ[par]
        avb = [BK[4 * par], BK[4 * par + 1]]
        trb = [2, 3] if par == 0 else [6, 7]

        transpose_blocks(
            lambda c, i=i: PFB[i][:, :, :].rearrange("p j k -> p (j k)")[:, c * 128:c * 128 + 128],
            [f"PFB{i}"],
            lambda c, i=i: PTS[i][:, c, :], [f"PTS{i}"], 8, trb)

        def em_av(eng, t=t, hg=hg, i=i, PB=PB):
            last = None
            for j in range(4):
                hkv = (hg * 4 + j) // 4
                for kb in range(2):
                    last = eng.matmul(
                        PB[64 * (j % 2):64 * (j % 2) + 64, (j // 2) * 512:(j // 2) * 512 + 128],
                        VTM[:, t + kb, hkv * 64:hkv * 64 + 64],
                        PTS[i][:, 2 * j + kb, :],
                        start=(kb == 0), stop=(kb == 1))
            return last
        S.add("pe", em_av, r=["VTM", f"PTS{i}"], w=avb)

        def em_oev(eng, t=t, hg=hg, PB=PB):
            eng.tensor_copy(OFMB[:, hg * 2, t * 128:t * 128 + 128], PB[:, 0:128])
            return eng.tensor_copy(OFMB[:, hg * 2 + 1, t * 128:t * 128 + 128], PB[:, 512:640])
        S.add("dve", em_oev, r=avb + ["OFMB"], w=["OFMB"])

    sc_stage1(0)
    for g in range(64):
        if g + 1 < 64:
            sc_stage1(g + 1)
        sc_stage2(g)

    ctxB2.close()
    ctxR4.close()
    S.phase_barrier()

    ctxB3 = contextlib.ExitStack()
    sbB3 = lambda n, s, dt: ctxB3.enter_context(nc.sbuf_tensor(n, list(s), dt))
    W1B = [sbB3(f"W1B{i}", [128, 8, 512], BF16) for i in range(2)]
    W2B = [sbB3(f"W2B{i}", [128, 4, 1024], BF16) for i in range(2)]
    HB = sbB3("HB", [128, 32, 512], BF16)

    w1bc = [0]
    w2bc = [0]
    for tb in range(4):
        for fs in range(8):
            j = w1bc[0] % 2
            w1bc[0] += 1
            wkey = f"W1B{j}"
            if tb == 0:
                dma("pool", W1B[j][:, :, :], w1b_s[fs].rearrange("c p x -> p c x"),
                    w=[wkey], stream=f"w1{j}")
                dma("sp", w1b_st.ap()[fs].rearrange("c p x -> p c x"), W1B[j][:, :, :],
                    r=[wkey], w=[f"stw1b_{fs}"], stream="sw1")
            else:
                dma("sp", W1B[j][:, :, :], w1b_st.ap()[fs].rearrange("c p x -> p c x"),
                    r=[f"stw1b_{fs}"], w=[wkey], stream=f"w1{j}")
            for fcl in range(4):
                fc = fs * 4 + fcl
                bki = fc % 4

                def em_h(eng, j=j, fcl=fcl, tb=tb, bki=bki):
                    last = None
                    for c in range(8):
                        last = eng.matmul(bank(bki)[:, :],
                                          W1B[j][:, c, fcl * 128:fcl * 128 + 128],
                                          OFMB[:, c, tb * 512:tb * 512 + 512],
                                          start=(c == 0), stop=(c == 7))
                    return last
                S.add("pe", em_h, r=[wkey, "OFMB"], w=[BK[bki]])

                def em_sil(eng, fc=fc, bki=bki):
                    return eng.activation(HB[:, fc, :], bank(bki)[:, :], AF.Silu,
                                          bias=0.0, scale=1.0)
                S.add("act", em_sil, r=[BK[bki], "HB"], w=["HB"])

        for fc in range(32):
            fg, fl = divmod(fc, 4)
            j2 = (w2bc[0] // 4) % 2
            w2bc[0] += 1
            w2key = f"W2B{j2}"
            if fl == 0:
                if tb == 0:
                    dma("pool", W2B[j2][:, :, :],
                        w2b_s.rearrange("f p o -> f p o")[4 * fg:4 * fg + 4].rearrange("f p o -> p f o"),
                        w=[w2key], stream=f"w2{j2}")
                    dma("sp", w2b_st.ap()[4 * fg:4 * fg + 4].rearrange("f p o -> p f o"),
                        W2B[j2][:, :, :], r=[w2key], w=[f"stw2b_{fg}"], stream="sw2")
                else:
                    dma("sp", W2B[j2][:, :, :],
                        w2b_st.ap()[4 * fg:4 * fg + 4].rearrange("f p o -> p f o"),
                        r=[f"stw2b_{fg}"], w=[w2key], stream=f"w2{j2}")

            def em_y(eng, j2=j2, fc=fc, fl=fl):
                last = None
                for tl in range(4):
                    for ns in range(2):
                        last = eng.matmul(PP[tl][:, ns * 512:ns * 512 + 512],
                                          HB[:, fc, tl * 128:tl * 128 + 128],
                                          W2B[j2][:, fl, ns * 512:ns * 512 + 512],
                                          start=(fc == 0), stop=(fc == 31))
                return last
            S.add("pe", em_y, r=[w2key, "HB"], w=ALLBK)

        for tl in range(4):
            t = tb * 4 + tl

            def em_xup(eng, t=t, tl=tl):
                return eng.tensor_add(X[:, t, :], X[:, t, :], PP[tl][:, :])
            S.add("dve", em_xup, r=PKEY[tl] + [f"X{t}"], w=[f"X{t}"])

    ctxB3.close()
    ctxR3.close()
    S.phase_barrier()
    if stop_phase <= 4:
        emit_output()
        S.emit_all(nc)
        return nc

    ffn_call(2)

    emit_output()
    S.emit_all(nc)
    return nc


def _host_prep(inputs):
    f = np.ascontiguousarray
    x = np.asarray(inputs["L0_attn_a_x"], dtype=np.float32)

    qkvT = f(np.asarray(inputs["L0_attn_a_qkv_w"], np.float32).T)
    projT = f(np.asarray(inputs["L0_attn_a_proj_w"], np.float32).T)
    conv1T = f(np.asarray(inputs["L0_attn_a_conv1_w"], np.float32).T)
    conv2T = f(np.asarray(inputs["L0_attn_a_conv2_w"], np.float32).T)
    scale1v = f(np.asarray(inputs["L0_attn_a_scale1"], np.float32).reshape(8, 128).T)
    scale2v = f(np.asarray(inputs["L0_attn_a_scale2"], np.float32).reshape(8, 128).T)
    gammaA = np.asarray(inputs["L0_attn_a_gamma"], np.float32).reshape(1, D)
    projb = np.asarray(inputs["L0_attn_a_proj_b"], np.float32).reshape(1, D)
    conv1b = f(np.asarray(inputs["L0_attn_a_conv1_b"], np.float32).reshape(16, 128).T)
    conv2b = np.asarray(inputs["L0_attn_a_conv2_b"], np.float32).reshape(1, D)
    WqT = f(np.asarray(inputs["L0_attn_b_Wq"], np.float32).T)
    WkT = np.asarray(inputs["L0_attn_b_Wk"], np.float32).T
    WkT = f(np.repeat(WkT.reshape(1024, 4, 64), 2, axis=1).reshape(1024, 512))
    WvT = f(np.asarray(inputs["L0_attn_b_Wv"], np.float32).T)
    w1b = np.asarray(inputs["L0_attn_b_W1"], np.float32)
    w2b = np.asarray(inputs["L0_attn_b_W2"], np.float32)
    w1b_s = f(w1b.reshape(8, 512, 8, 128).transpose(0, 2, 3, 1))
    w2b_s = f(w2b.reshape(1024, 32, 128).transpose(1, 2, 0))
    gammaF = f(np.asarray(inputs["L0_ffn_gamma"], np.float32).reshape(8, 128).T)
    betaF = f(np.asarray(inputs["L0_ffn_beta"], np.float32).reshape(8, 128).T)
    WgT = f(np.asarray(inputs["L0_ffn_Wg"], np.float32).T)
    bg = np.asarray(inputs["L0_ffn_bg"], np.float32).reshape(1, 8)
    b1 = np.asarray(inputs["L0_ffn_b1"], np.float32)
    b1v = f(b1.reshape(8, 32, 128).transpose(2, 0, 1))
    b2r = f(np.asarray(inputs["L0_ffn_b2"], np.float32).reshape(1, 8192))
    w1 = np.asarray(inputs["L0_ffn_W1"], np.float32)
    w2 = np.asarray(inputs["L0_ffn_W2"], np.float32)
    w1_s = f(w1.reshape(8, 8, 512, 8, 128).transpose(0, 1, 3, 4, 2))
    w2_s = f(w2.transpose(0, 2, 1).reshape(8, 32, 128, 1024))

    identf = np.eye(128, dtype=np.float32)
    ones128 = np.ones((1, 128), dtype=np.float32)
    cpool = np.zeros((128, 8), dtype=np.float32)
    cpool[:, 0] = -0.5
    cpool[:, 1] = C_SQ
    cpool[:, 2] = 1e-6

    r_i = np.arange(128)[:, None]
    c_i = np.arange(256)[None, :]
    band = (c_i >= r_i + 1) & (c_i <= r_i + 128)
    mstd = np.where(band, 0.0, NEG).astype(np.float32)
    m0 = np.where(band & (c_i >= 128), 0.0, NEG).astype(np.float32)
    masks_first = f(np.stack([m0, mstd], axis=1))
    masks_mid = f(np.stack([mstd, mstd], axis=1))

    inv_freq = (1.0 / (10000.0 ** (np.arange(0, 64, 2, dtype=np.float32) / 64.0)))

    shared = dict(qkvT=qkvT, projT=projT, conv1T=conv1T, conv2T=conv2T,
                  scale1v=scale1v, scale2v=scale2v, gammaA=gammaA, projb=projb,
                  conv1b=conv1b, conv2b=conv2b, WqT=WqT, WkT=WkT, WvT=WvT,
                  w1b_s=w1b_s, w2b_s=w2b_s, gammaF=gammaF, betaF=betaF,
                  WgT=WgT, bg=bg, b1v=b1v, b2r=b2r, w1_s=w1_s, w2_s=w2_s,
                  identf=identf, ones128=ones128, cpool=cpool)

    in_maps = []
    for core in range(8):
        b = core // 4
        s0 = (core % 4) * T_CORE
        m = dict(shared)
        m["x"] = f(x[b, s0:s0 + T_CORE, :])
        m["masks"] = masks_first if core % 4 == 0 else masks_mid
        pos = np.maximum(np.arange(s0 - 128, s0 + T_CORE, dtype=np.float32), 0.0)
        freqs = pos[:, None] * inv_freq[None, :]
        m["ropeC"] = f(np.cos(freqs).reshape(17, 128, 32)).astype(np.float32)
        m["ropeS"] = f(np.sin(freqs).reshape(17, 128, 32)).astype(np.float32)
        hsel = np.zeros((128, 8), dtype=np.float32)
        hsel[:, (core - 1) % 8] = 1.0
        m["halo_sel"] = hsel
        in_maps.append(m)
    return in_maps


_CACHE = {}


def kernel(**inputs):
    in_maps = _host_prep(inputs)
    if "nc" not in _CACHE:
        _CACHE["nc"] = build_program()
    nc = _CACHE["nc"]
    res = run_bass_kernel_spmd(nc, in_maps, core_ids=list(range(8)))
    shards = [res.results[c]["y"] for c in range(8)]
    out = np.stack([np.concatenate(shards[0:4], axis=0),
                    np.concatenate(shards[4:8], axis=0)], axis=0)
    return out.astype(np.float32)
